# version 20
# baseline (speedup 1.0000x reference)
import sys

for _p in ("/opt/trn_rl_repo",):
    if _p not in sys.path:
        sys.path.insert(0, _p)

import numpy as np
import ml_dtypes
from contextlib import ExitStack

import concourse.bass as bass
import concourse.mybir as mybir
import concourse.bass_utils as _bass_utils
from concourse.bass_utils import run_bass_kernel_spmd

_orig_bvo = _bass_utils.bir_verify_and_optimise


def _bvo_ldw(*args, **kwargs):
    import subprocess as _sp
    orig_run = _bass_utils.run_command

    def patched_run(cmd, **kw):
        cmd = [c.replace("--enable-ldw-opt=false", "--enable-ldw-opt=true")
               if isinstance(c, str) else c for c in cmd]
        return orig_run(cmd, **kw)

    _bass_utils.run_command = patched_run
    try:
        return _orig_bvo(*args, **kwargs)
    finally:
        _bass_utils.run_command = orig_run


_bass_utils.bir_verify_and_optimise = _bvo_ldw

F32 = mybir.dt.float32
BF16 = mybir.dt.bfloat16
I32 = mybir.dt.int32
AF = mybir.ActivationFunctionType
ALU = mybir.AluOpType
AX = mybir.AxisListType.X

NCORES = 8
B, S, D, E, FF = 8, 4096, 1024, 8, 4096
TC = B * S // NCORES
TT = 512
NT = TC // TT
NSUB = TT // 128
KD = D // 128
NF = FF // 128
GC1 = 0.071354816
GC2 = 1.5957691216


class Sched:

    ENGINES = ("t", "v", "s", "g", "y")
    SELF_EDGES = {"t": False, "v": True, "s": True, "g": True, "y": True}
    SELF_SKIP_MARGIN = {"v": 16, "s": 16}
    INC = {"t": 1, "v": 1, "s": 1, "g": 16, "y": 16}
    DMA_SEMS = ("gc", "gx", "gy", "kq", "b1a", "b1b", "w1a", "w1b", "w2a", "w2b",
                "s1a", "s1b", "s2a", "s2b", "c1a", "c1b", "c2a", "c2b", "dbg")

    def __init__(self):
        self.prog = {e: [] for e in self.ENGINES}
        self.cnt = {e: 0 for e in self.ENGINES}
        for smm in self.DMA_SEMS:
            self.cnt[smm] = 0
        self.vc = {e: {} for e in self.ENGINES}
        self.snap = {e: {} for e in self.ENGINES}
        for smm in self.DMA_SEMS:
            self.snap[smm] = {}
        self.writers = {}
        self.readers = {}

    @staticmethod
    def _overlap(a, b):
        return a[0] < b[1] and b[0] < a[1] and a[2] < b[3] and b[2] < a[3]

    @staticmethod
    def _covers(a, b):
        return a[0] <= b[0] and a[1] >= b[1] and a[2] <= b[2] and a[3] >= b[3]

    def _need_from(self, regions, want_writers, want_readers, need):
        for r in regions:
            key, ival = r[0], r[1:]
            if want_writers:
                for w in self.writers.get(key, ()):
                    if self._overlap(ival, w[:4]):
                        e, c = w[4], w[5]
                        if need.get(e, 0) < c:
                            need[e] = c
            if want_readers:
                for (rival, e), c in self.readers.get(key, {}).items():
                    if self._overlap(ival, rival):
                        if need.get(e, 0) < c:
                            need[e] = c

    def _snap_at(self, sem, cnt):
        sn = self.snap[sem]
        best = None
        bk = -1
        for k, v in sn.items():
            if bk < k <= cnt:
                bk, best = k, v
        return best

    def op(self, eng, emit, reads=(), writes=(), inc=None, sem=None, defer_commit=False):
        mysem = sem if sem is not None else eng
        need = {}
        self._need_from(reads, True, False, need)
        self._need_from(writes, True, True, need)
        vc = self.vc[eng]
        margin = self.SELF_SKIP_MARGIN.get(eng)
        waits = []
        for dsem, dcnt in sorted(need.items()):
            if dsem == mysem and mysem == eng:
                if not self.SELF_EDGES[eng]:
                    continue
                if margin is not None and self.cnt[eng] - dcnt >= margin:
                    continue
            if vc.get(dsem, 0) >= dcnt:
                continue
            waits.append((dsem, dcnt))
        for dsem, dcnt in waits:
            if vc.get(dsem, 0) < dcnt:
                vc[dsem] = dcnt
            sn = self._snap_at(dsem, dcnt)
            if sn:
                for k, v in sn.items():
                    if vc.get(k, 0) < v:
                        vc[k] = v
        step = self.INC[eng] if inc is None else inc
        self.cnt[mysem] += step
        newcnt = self.cnt[mysem]
        self.prog[eng].append((tuple(waits), emit, mysem, step))
        sn = dict(vc)
        sn[mysem] = newcnt
        self.snap[mysem][newcnt] = sn
        if not defer_commit:
            self.commit(mysem, newcnt, reads, writes)
        return newcnt

    def commit(self, mysem, newcnt, reads, writes):
        for r in reads:
            key, ival = r[0], r[1:]
            rd = self.readers.setdefault(key, {})
            k2 = (ival, mysem)
            if rd.get(k2, 0) < newcnt:
                rd[k2] = newcnt
        for r in writes:
            key, ival = r[0], r[1:]
            ws = self.writers.setdefault(key, [])
            ws[:] = [w for w in ws if not self._covers(ival, w[:4])]
            ws.append((*ival, mysem, newcnt))
            rd = self.readers.get(key)
            if rd:
                for k2 in [k for k in rd if self._covers(ival, k[0])]:
                    del rd[k2]

    def raw_wait(self, eng, dsem, dcnt):
        self.prog[eng].append((((dsem, dcnt),), None, eng, 0))

    def replay(self, eng, engine, sems):
        for waits, emit, mysem, step in self.prog[eng]:
            for dsem, dcnt in waits:
                engine.wait_ge(sems[dsem], dcnt)
            if emit is not None:
                ins = emit(engine)
                if step:
                    ins.then_inc(sems[mysem], step)


def R(t, p0, p1, f0, f1):
    return (id(t), p0, p1, f0, f1)


def dr3(dram_slice, p=128):
    return dram_slice.rearrange("(a p) c -> p a c", p=p)


def sb3(sbuf_ap, a):
    return sbuf_ap.rearrange("p (a c) -> p a c", a=a)


def build_program(nt=NT, ne=E, do_router=True, do_ffn=True, do_quant=True, do_b1prep=True, do_smalls=True, do_debug=False):
    nc = bass.Bass("TRN2", target_bir_lowering=False)

    x_d = nc.dram_tensor("x", [TC, D], F32, kind="ExternalInput").ap()
    gate_d = nc.dram_tensor("gate", [D, E], F32, kind="ExternalInput").ap()
    w1_d = nc.dram_tensor("w1", [E, D, FF], F32, kind="ExternalInput").ap()
    b1_d = nc.dram_tensor("b1", [E, FF], F32, kind="ExternalInput").ap()
    w2_d = nc.dram_tensor("w2", [E, FF, D], F32, kind="ExternalInput").ap()
    b2_d = nc.dram_tensor("b2", [E, D], F32, kind="ExternalInput").ap()
    fk_d = nc.dram_tensor("fk", [D, D], F32, kind="ExternalInput").ap()
    fb_d = nc.dram_tensor("fb", [1, D], F32, kind="ExternalInput").ap()
    fa_d = nc.dram_tensor("fa", [D, 16], F32, kind="ExternalInput").ap()
    fbb_d = nc.dram_tensor("fbb", [16, D], F32, kind="ExternalInput").ap()
    identb_d = nc.dram_tensor("identb", [128, 128], BF16, kind="ExternalInput").ap()
    identf_d = nc.dram_tensor("identf", [128, 128], F32, kind="ExternalInput").ap()
    consts_d = nc.dram_tensor("consts", [128, 4], F32, kind="ExternalInput").ap()
    onesf_d = nc.dram_tensor("onesf", [1, 512], F32, kind="ExternalInput").ap()
    y_d = nc.dram_tensor("y", [TC, D], F32, kind="ExternalOutput").ap()
    dbg = {}
    if do_debug:
        for nm, shape, dt in [("dbg_comb", [128, 32], F32), ("dbg_e", [128, 32], F32),
                              ("dbg_combT", [8, 512], F32), ("dbg_acc", [128, 4096], F32),
                              ("dbg_x1", [128, 4096], F32), ("dbg_lnT", [128, 4096], F32),
                              ("dbg_hT", [128, 2048], F32), ("dbg_rsc", [128, 16], F32),
                              ("dbg_wq", [128, 8192], F32), ("dbg_qsc", [128, 2], F32),
                              ("dbg_rT", [16, 512], F32)]:
            dbg[nm] = nc.dram_tensor(nm, shape, dt, kind="ExternalOutput").ap()
    w1bf_d = nc.dram_tensor("w1bf", [E, D, FF], BF16, kind="Internal").ap()
    w2bf_d = nc.dram_tensor("w2bf", [E, FF, D], BF16, kind="Internal").ap()

    es = ExitStack()
    with es:
        sb = lambda name, shape, dt: es.enter_context(nc.sbuf_tensor(name, shape, dt))
        ps = lambda name, shape, dt: es.enter_context(nc.psum_tensor(name, shape, dt))

        xb = [sb("x_sbA", [128, NSUB * D], F32), sb("x_sbB", [128, NSUB * D], F32)]
        out_sb = sb("out_sb", [128, NSUB * D], F32)
        acc_sb = sb("acc_sb", [128, NSUB * D], F32)
        lnst = sb("lnst", [128, NSUB * D], BF16)
        lnT = sb("lnT", [128, KD * TT], BF16)
        hT = sb("hT", [128, NF * TT], BF16)
        w1b = [sb(f"w1b{i}", [128, KD * 1024], BF16) for i in range(2)]
        w2b = [sb(f"w2b{i}", [128, 8 * 512], BF16) for i in range(2)]
        wqb = sb("wqb", [128, KD * 1024], BF16)
        kstage = sb("kstage", [128, 2048], F32)
        xg = sb("xg", [128, 2 * 512], BF16)
        ug = sb("ug", [128, 2 * 512], BF16)
        vg = sb("vg", [128, 2 * 512], BF16)
        b1st = sb("b1st", [8, 2 * 128], F32)
        b1col = sb("b1col", [128, NF * E], F32)
        gate_bf = sb("gate_bf", [128, KD * E], BF16)
        b2_bf = sb("b2_bf", [8, D], BF16)
        a_bf = sb("a_bf", [128, KD * 16], BF16)
        bb_bf = sb("bb_bf", [16, D], BF16)
        fbrow = sb("fbrow", [1, D], BF16)
        onesb = sb("onesb", [1, 512], BF16)
        identb_sb = sb("identb_sb", [128, 128], BF16)
        identf_sb = sb("identf_sb", [128, 128], F32)
        consts_sb = sb("consts_sb", [128, 4], F32)
        stats = sb("stats", [128, 12], F32)
        mv = sb("mv", [128, 2], F32)
        lnsc = sb("lnsc", [128, 4], F32)
        e_sb = sb("e_sb", [128, NSUB * E], F32)
        rsc = sb("rsc", [128, NSUB * 4], F32)
        rsc2 = sb("rsc2", [128, NSUB * 4], F32)
        ris = sb("ris", [128, E], F32)
        rwo = sb("rwo", [128, E], F32)
        rkeep = sb("rkeep", [128, E], F32)
        comb = sb("comb", [128, NSUB * E], F32)
        comb_bf = sb("comb_bf", [128, NSUB * E], BF16)
        combT = sb("combT", [8, NSUB * 128], BF16)
        rT_sb = sb("rT_sb", [16, TT], BF16)
        qsc = sb("qsc", [128, 2], F32)
        qrow = sb("qrow", [1, 4], F32)
        qhi = sb("qhi", [1, 2], BF16)
        qhf = sb("qhf", [1, 2], F32)
        qdf = sb("qdf", [1, 2], F32)
        qlo = sb("qlo", [1, 2], BF16)
        cmax = sb("cmax", [128, 4], F32)
        qmax = sb("qmax", [128, 1], F32)

        pb = [ps(f"pb{i}", [128, 512], F32) for i in range(2)]
        py = [ps(f"py{i}", [128, 512], F32) for i in range(4)]
        ptrs = [ps("ptra", [128, 1024], BF16), ps("ptrb", [128, 1024], BF16)]

        sems = {}
        for nm in Sched.ENGINES + Sched.DMA_SEMS:
            sems[nm] = es.enter_context(nc.semaphore(f"sem_{nm}"))

        sc = Sched()

        def gdma(dst_ap, src_ap, reads, writes, sem="gx", defer_commit=False):
            return sc.op("g", lambda g, d=dst_ap, s=src_ap: g.dma_start(d, s),
                         reads=reads, writes=writes, sem=sem, defer_commit=defer_commit)

        def ydma(dst_ap, src_ap, reads, writes, sem="kq"):
            return sc.op("y", lambda y, d=dst_ap, s=src_ap: y.dma_start(d, s),
                         reads=reads, writes=writes, sem=sem)

        DR = lambda t, i0=0, i1=1: (("dram", id(t)), 0, 1, i0, i1)

        const_writes = []

        def cdma(dst_ap, src_ap, rd, wr):
            gdma(dst_ap, src_ap, rd, wr, sem="gc", defer_commit=True)
            const_writes.extend(wr)

        cdma(identb_sb[:, :], identb_d, [DR(identb_d)], [R(identb_sb, 0, 128, 0, 128)])
        cdma(identf_sb[:, :], identf_d, [DR(identf_d)], [R(identf_sb, 0, 128, 0, 128)])
        cdma(consts_sb[:, :], consts_d, [DR(consts_d)], [R(consts_sb, 0, 128, 0, 4)])
        cdma(onesb[:, :], onesf_d, [DR(onesf_d)], [R(onesb, 0, 1, 0, 512)])
        if do_smalls:
            cdma(sb3(gate_bf[:, :], KD), dr3(gate_d), [DR(gate_d)], [R(gate_bf, 0, 128, 0, KD * E)])
            cdma(b2_bf[:, :], b2_d, [DR(b2_d)], [R(b2_bf, 0, 8, 0, D)])
            cdma(sb3(a_bf[:, :], KD), dr3(fa_d), [DR(fa_d)], [R(a_bf, 0, 128, 0, KD * 16)])
            cdma(bb_bf[:, :], fbb_d, [DR(fbb_d)], [R(bb_bf, 0, 16, 0, D)])
            cdma(fbrow[:, :], fb_d, [DR(fb_d)], [R(fbrow, 0, 1, 0, D)])
        sc.commit("gc", sc.cnt["gc"], [], const_writes)

        def preamble_b1():
            for f in range(NF if do_b1prep else 0):
                par = f % 2
                gdma(b1st[:, par * 128:(par + 1) * 128],
                     b1_d[:, f * 128:(f + 1) * 128],
                     [DR(b1_d)], [R(b1st, 0, 8, par * 128, (par + 1) * 128)],
                     sem=("b1a", "b1b")[par])
                slot = f % 8
                sc.op("t", lambda t, par=par, slot=slot: t.transpose(
                    pb[0][:, slot * 8:(slot + 1) * 8],
                    b1st[0:8, par * 128:(par + 1) * 128],
                    identf_sb[0:8, 0:8]),
                    reads=[R(b1st, 0, 8, par * 128, (par + 1) * 128),
                           R(identf_sb, 0, 8, 0, 8)],
                    writes=[R(pb[0], 0, 128, slot * 8, (slot + 1) * 8)])
                if slot == 7:
                    grp = f // 8
                    sc.op("s", lambda s, grp=grp: s.activation(
                        b1col[:, grp * 64:(grp + 1) * 64], pb[0][:, 0:64],
                        AF.Identity, bias=0.0, scale=1.0),
                        reads=[R(pb[0], 0, 128, 0, 64)],
                        writes=[R(b1col, 0, 128, grp * 64, (grp + 1) * 64)])

        def preamble_quant():
            def load_fk_quarter(h):
                ydma(sb3(kstage[:, :], 2), dr3(fk_d[h * 256:(h + 1) * 256, :]),
                     [DR(fk_d)], [R(kstage, 0, 128, 0, 2048)])

            for h in range(4 if do_quant else 0):
                load_fk_quarter(h)
                sc.op("v", lambda v, h=h: v.reduce_max(
                    cmax[:, h:h + 1], kstage[:, :], axis=AX, apply_absolute_value=True),
                    reads=[R(kstage, 0, 128, 0, 2048)],
                    writes=[R(cmax, 0, 128, h, h + 1)])
            if not do_quant:
                sc.op("v", lambda v: v.memset(wqb[:, :], 0.0), reads=[], writes=[R(wqb, 0, 128, 0, KD * 1024)])
            if do_quant:
              sc.op("v", lambda v: v.tensor_tensor(qmax[:, 0:1], cmax[:, 0:1], cmax[:, 1:2], ALU.max),
                    reads=[R(cmax, 0, 128, 0, 2)], writes=[R(qmax, 0, 128, 0, 1)])
              sc.op("v", lambda v: v.tensor_tensor(cmax[:, 2:3], cmax[:, 2:3], cmax[:, 3:4], ALU.max),
                    reads=[R(cmax, 0, 128, 2, 4)], writes=[R(cmax, 0, 128, 2, 3)])
              sc.op("v", lambda v: v.tensor_tensor(qmax[:, 0:1], qmax[:, 0:1], cmax[:, 2:3], ALU.max),
                    reads=[R(qmax, 0, 128, 0, 1), R(cmax, 0, 128, 2, 3)],
                    writes=[R(qmax, 0, 128, 0, 1)])
              sc.op("t", lambda t: t.transpose(pb[0][0:1, 0:128], qmax[:, 0:1], identf_sb[:, :]),
                    reads=[R(qmax, 0, 128, 0, 1), R(identf_sb, 0, 128, 0, 128)],
                    writes=[R(pb[0], 0, 1, 0, 128)])
              sc.op("v", lambda v: v.reduce_max(qrow[0:1, 0:1], pb[0][0:1, 0:128], axis=AX),
                    reads=[R(pb[0], 0, 1, 0, 128)], writes=[R(qrow, 0, 1, 0, 1)])
              sc.op("v", lambda v: v.reciprocal(qrow[0:1, 1:2], qrow[0:1, 0:1]),
                    reads=[R(qrow, 0, 1, 0, 1)], writes=[R(qrow, 0, 1, 1, 2)])
              sc.op("v", lambda v: v.tensor_scalar(qrow[0:1, 2:3], qrow[0:1, 1:2], 7.0, None, ALU.mult),
                    reads=[R(qrow, 0, 1, 1, 2)], writes=[R(qrow, 0, 1, 2, 3)])
              sc.op("v", lambda v: v.tensor_scalar(qrow[0:1, 3:4], qrow[0:1, 0:1], 1.0 / 7.0, None, ALU.mult),
                    reads=[R(qrow, 0, 1, 0, 1)], writes=[R(qrow, 0, 1, 3, 4)])
              sc.op("v", lambda v: v.tensor_copy(qhi[0:1, 0:2], qrow[0:1, 2:4]),
                    reads=[R(qrow, 0, 1, 2, 4)], writes=[R(qhi, 0, 1, 0, 2)])
              sc.op("v", lambda v: v.tensor_copy(qhf[0:1, 0:2], qhi[0:1, 0:2]),
                    reads=[R(qhi, 0, 1, 0, 2)], writes=[R(qhf, 0, 1, 0, 2)])
              sc.op("v", lambda v: v.tensor_tensor(qdf[0:1, 0:2], qrow[0:1, 2:4], qhf[0:1, 0:2], ALU.subtract),
                    reads=[R(qrow, 0, 1, 2, 4), R(qhf, 0, 1, 0, 2)], writes=[R(qdf, 0, 1, 0, 2)])
              sc.op("v", lambda v: v.tensor_copy(qlo[0:1, 0:2], qdf[0:1, 0:2]),
                    reads=[R(qdf, 0, 1, 0, 2)], writes=[R(qlo, 0, 1, 0, 2)])

              def qbcast(t):
                  t.matmul(pb[1][:, 0:2], onesb[0:1, 0:128], qhi[0:1, 0:2], start=True, stop=False)
                  return t.matmul(pb[1][:, 0:2], onesb[0:1, 0:128], qlo[0:1, 0:2], start=False, stop=True)

              sc.op("t", qbcast,
                    reads=[R(onesb, 0, 1, 0, 128), R(qhi, 0, 1, 0, 2), R(qlo, 0, 1, 0, 2)],
                    writes=[R(pb[1], 0, 128, 0, 2)])
              sc.op("s", lambda s: s.activation(qsc[:, :], pb[1][:, 0:2], AF.Identity, bias=0.0, scale=1.0),
                    reads=[R(pb[1], 0, 128, 0, 2)], writes=[R(qsc, 0, 128, 0, 2)])

              for h in range(4):
                  load_fk_quarter(h)
                  kr = [R(kstage, 0, 128, 0, 2048)]
                  kw = kr
                  sc.op("v", lambda v: v.tensor_scalar(kstage[:, :], kstage[:, :], qsc[:, 0:1], None, ALU.mult),
                        reads=kr + [R(qsc, 0, 128, 0, 1)], writes=kw)
                  sc.op("v", lambda v: v.tensor_scalar(kstage[:, :], kstage[:, :], 12582912.0, None, ALU.add),
                        reads=kr, writes=kw)
                  sc.op("v", lambda v: v.tensor_scalar(kstage[:, :], kstage[:, :], -12582912.0, None, ALU.add),
                        reads=kr, writes=kw)
                  sc.op("v", lambda v: v.tensor_scalar(kstage[:, :], kstage[:, :], 7.0, None, ALU.min),
                        reads=kr, writes=kw)
                  sc.op("v", lambda v: v.tensor_scalar(kstage[:, :], kstage[:, :], -8.0, None, ALU.max),
                        reads=kr, writes=kw)
                  sc.op("v", lambda v, h=h: v.tensor_scalar(
                      wqb[:, h * 2048:(h + 1) * 2048], kstage[:, :], qsc[:, 1:2], None, ALU.mult),
                      reads=kr + [R(qsc, 0, 128, 1, 2)],
                      writes=[R(wqb, 0, 128, h * 2048, (h + 1) * 2048)])


        def layernorm(i, xbuf):
            xr = R(xbuf, 0, 128, i * D, (i + 1) * D)
            l0 = i * D
            sc.op("v", lambda v, i=i, xbuf=xbuf: v.bn_stats(stats[:, 0:6], xbuf[:, i * D:i * D + 512]),
                  reads=[xr], writes=[R(stats, 0, 128, 0, 6)])
            sc.op("v", lambda v, i=i, xbuf=xbuf: v.bn_stats(stats[:, 6:12], xbuf[:, i * D + 512:(i + 1) * D]),
                  reads=[xr], writes=[R(stats, 0, 128, 6, 12)])
            sc.op("v", lambda v: v.bn_aggr(mv[:, :], stats[:, :]),
                  reads=[R(stats, 0, 128, 0, 12)], writes=[R(mv, 0, 128, 0, 2)])
            sc.op("s", lambda s: s.activation(lnsc[:, 0:1], mv[:, 0:1], AF.Identity, bias=0.0, scale=1.0),
                  reads=[R(mv, 0, 128, 0, 1)], writes=[R(lnsc, 0, 128, 0, 1)])
            sc.op("s", lambda s: s.activation(lnsc[:, 1:2], mv[:, 1:2], AF.Ln, bias=consts_sb[:, 0:1], scale=1.0),
                  reads=[R(mv, 0, 128, 1, 2), R(consts_sb, 0, 128, 0, 1)],
                  writes=[R(lnsc, 0, 128, 1, 2)])
            sc.op("s", lambda s: s.activation(lnsc[:, 2:3], lnsc[:, 1:2], AF.Exp, bias=0.0, scale=consts_sb[:, 1:2]),
                  reads=[R(lnsc, 0, 128, 1, 2), R(consts_sb, 0, 128, 1, 2)],
                  writes=[R(lnsc, 0, 128, 2, 3)])
            sc.op("v", lambda v, i=i, xbuf=xbuf, l0=l0: v.tensor_scalar(
                lnst[:, l0:l0 + D], xbuf[:, i * D:(i + 1) * D], lnsc[:, 0:1], lnsc[:, 2:3],
                ALU.subtract, ALU.mult),
                reads=[xr, R(lnsc, 0, 128, 0, 1), R(lnsc, 0, 128, 2, 3)],
                writes=[R(lnst, 0, 128, l0, l0 + D)])
            for k in range(KD):
                pt = ptrs[k % 2]
                sc.op("t", lambda t, k=k, pt=pt, l0=l0: t.transpose(
                    pt[:, 0:128], lnst[:, l0 + k * 128:l0 + (k + 1) * 128], identb_sb[:, :]),
                    reads=[R(lnst, 0, 128, l0 + k * 128, l0 + (k + 1) * 128), R(identb_sb, 0, 128, 0, 128)],
                    writes=[R(pt, 0, 128, 0, 1024)])
                sc.op("s", lambda s, k=k, i=i, pt=pt: s.activation(
                    lnT[:, k * TT + i * 128: k * TT + (i + 1) * 128],
                    pt[:, 0:128], AF.Identity, bias=0.0, scale=1.0),
                    reads=[R(pt, 0, 128, 0, 1024)],
                    writes=[R(lnT, 0, 128, k * TT + i * 128, k * TT + (i + 1) * 128)])

        def router(i):
            def rmm(t, i=i):
                ins = None
                for k in range(KD):
                    ins = t.matmul(py[3][:, 0:8],
                                   lnT[:, k * TT + i * 128: k * TT + (i + 1) * 128],
                                   gate_bf[:, k * E:(k + 1) * E],
                                   start=(k == 0), stop=(k == KD - 1))
                return ins
            sc.op("t", rmm,
                  reads=[R(lnT, 0, 128, k * TT + i * 128, k * TT + (i + 1) * 128) for k in range(KD)]
                        + [R(gate_bf, 0, 128, 0, KD * E)],
                  writes=[R(py[3], 0, 128, 0, 512)])
            sc.op("v", lambda v: v.reduce_max(lnsc[:, 3:4], py[3][:, 0:8], axis=AX, negate=True),
                  reads=[R(py[3], 0, 128, 0, 512)], writes=[R(lnsc, 0, 128, 3, 4)])
            ei0, ei1 = i * E, (i + 1) * E
            sc.op("s", lambda s, ei0=ei0, ei1=ei1: s.activation(
                e_sb[:, ei0:ei1], py[3][:, 0:8], AF.Exp, bias=lnsc[:, 3:4], scale=1.0),
                reads=[R(py[3], 0, 128, 0, 512), R(lnsc, 0, 128, 3, 4)],
                writes=[R(e_sb, 0, 128, ei0, ei1)])
            c0 = i * 4
            sc.op("v", lambda v, ei0=ei0, ei1=ei1, c0=c0: v.reduce_max(
                rsc[:, c0:c0 + 1], e_sb[:, ei0:ei1], axis=AX),
                reads=[R(e_sb, 0, 128, ei0, ei1)], writes=[R(rsc, 0, 128, c0, c0 + 1)])
            sc.op("s", lambda s, c0=c0: s.activation(
                rsc2[:, c0:c0 + 1], rsc[:, c0:c0 + 1], AF.Identity, bias=0.0, scale=1.0),
                reads=[R(rsc, 0, 128, c0, c0 + 1)], writes=[R(rsc2, 0, 128, c0, c0 + 1)])
            sc.op("v", lambda v, ei0=ei0, ei1=ei1, c0=c0: v.tensor_scalar(
                ris[:, :], e_sb[:, ei0:ei1], rsc2[:, c0:c0 + 1], None, ALU.is_ge),
                reads=[R(e_sb, 0, 128, ei0, ei1), R(rsc2, 0, 128, c0, c0 + 1)],
                writes=[R(ris, 0, 128, 0, E)])
            sc.op("v", lambda v, ei0=ei0, ei1=ei1: v.scalar_tensor_tensor(
                rwo[:, :], ris[:, :], -2.0, e_sb[:, ei0:ei1], ALU.mult, ALU.add),
                reads=[R(ris, 0, 128, 0, E), R(e_sb, 0, 128, ei0, ei1)],
                writes=[R(rwo, 0, 128, 0, E)])
            sc.op("v", lambda v, c0=c0: v.reduce_max(rsc[:, c0 + 1:c0 + 2], rwo[:, :], axis=AX),
                  reads=[R(rwo, 0, 128, 0, E)], writes=[R(rsc, 0, 128, c0 + 1, c0 + 2)])
            sc.op("v", lambda v, c0=c0: v.tensor_tensor(
                rsc[:, c0 + 2:c0 + 3], rsc[:, c0:c0 + 1], rsc[:, c0 + 1:c0 + 2], ALU.add),
                reads=[R(rsc, 0, 128, c0, c0 + 2)], writes=[R(rsc, 0, 128, c0 + 2, c0 + 3)])
            sc.op("v", lambda v, c0=c0: v.reciprocal(rsc[:, c0 + 3:c0 + 4], rsc[:, c0 + 2:c0 + 3]),
                  reads=[R(rsc, 0, 128, c0 + 2, c0 + 3)], writes=[R(rsc, 0, 128, c0 + 3, c0 + 4)])
            sc.op("s", lambda s, c0=c0: s.activation(
                rsc2[:, c0 + 1:c0 + 4], rsc[:, c0 + 1:c0 + 4], AF.Identity, bias=0.0, scale=1.0),
                reads=[R(rsc, 0, 128, c0 + 1, c0 + 4)], writes=[R(rsc2, 0, 128, c0 + 1, c0 + 4)])
            sc.op("v", lambda v, ei0=ei0, ei1=ei1, c0=c0: v.tensor_scalar(
                rkeep[:, :], e_sb[:, ei0:ei1], rsc2[:, c0 + 1:c0 + 2], None, ALU.is_ge),
                reads=[R(e_sb, 0, 128, ei0, ei1), R(rsc2, 0, 128, c0 + 1, c0 + 2)],
                writes=[R(rkeep, 0, 128, 0, E)])
            sc.op("v", lambda v, ei0=ei0, ei1=ei1, c0=c0: v.scalar_tensor_tensor(
                comb[:, ei0:ei1], e_sb[:, ei0:ei1], rsc2[:, c0 + 3:c0 + 4], rkeep[:, :],
                ALU.mult, ALU.mult),
                reads=[R(e_sb, 0, 128, ei0, ei1), R(rsc2, 0, 128, c0 + 3, c0 + 4),
                       R(rkeep, 0, 128, 0, E)],
                writes=[R(comb, 0, 128, ei0, ei1)])
            sc.op("v", lambda v, ei0=ei0, ei1=ei1: v.tensor_copy(comb_bf[:, ei0:ei1], comb[:, ei0:ei1]),
                  reads=[R(comb, 0, 128, ei0, ei1)], writes=[R(comb_bf, 0, 128, ei0, ei1)])

        def router_fin(i):
            ei0, ei1 = i * E, (i + 1) * E
            pt = ptrs[i % 2]
            sc.op("t", lambda t, ei0=ei0, ei1=ei1, i=i, pt=pt: t.transpose(
                pt[0:8, 0:128], comb_bf[:, ei0:ei1], identb_sb[:, :]),
                reads=[R(comb_bf, 0, 128, ei0, ei1), R(identb_sb, 0, 128, 0, 128)],
                writes=[R(pt, 0, 128, 0, 1024)])
            sc.op("s", lambda s, i=i, pt=pt: s.activation(
                combT[0:8, i * 128:(i + 1) * 128], pt[0:8, 0:128],
                AF.Identity, bias=0.0, scale=1.0),
                reads=[R(pt, 0, 128, 0, 1024)],
                writes=[R(combT, 0, 8, i * 128, (i + 1) * 128)])

        def b2_init():
            for idx, (i, n) in enumerate([(i, n) for i in range(NSUB) for n in range(2)]):
                bank = py[(2 * i + n) % 4]
                sc.op("t", lambda t, i=i, n=n, bank=bank: t.matmul(
                    bank[:, 0:512], combT[0:8, i * 128:(i + 1) * 128],
                    b2_bf[0:8, n * 512:(n + 1) * 512], start=True, stop=True),
                    reads=[R(combT, 0, 8, i * 128, (i + 1) * 128), R(b2_bf, 0, 8, n * 512, (n + 1) * 512)],
                    writes=[R(bank, 0, 128, 0, 512)])
                sc.op("s", lambda s, i=i, n=n, bank=bank: s.activation(
                    acc_sb[:, i * D + n * 512: i * D + (n + 1) * 512], bank[:, 0:512],
                    AF.Identity, bias=0.0, scale=1.0),
                    reads=[R(bank, 0, 128, 0, 512)],
                    writes=[R(acc_sb, 0, 128, i * D + n * 512, i * D + (n + 1) * 512)])

        w1_chunk_idx = [0]
        w2_chunk_idx = [0]

        def load_w1_chunk(t, e, q):
            par = w1_chunk_idx[0] % 2
            w1_chunk_idx[0] += 1
            dst = w1b[par]
            wreg = [R(dst, 0, 128, 0, KD * 1024)]
            bsem = ("w1a", "w1b")[par]
            if t == 0:
                src = dr3(w1_d[e, :, q * 1024:(q + 1) * 1024])
                gdma(sb3(dst[:, :], KD), src, [DR(w1_d, e * 4 + q, e * 4 + q + 1)], wreg,
                     sem=("c1a", "c1b")[par])
                sdst = dr3(w1bf_d[e, :, q * 1024:(q + 1) * 1024])
                ydma(sdst, sb3(dst[:, :], KD), wreg, [DR(w1bf_d, e * 4 + q, e * 4 + q + 1)],
                     sem=("s1a", "s1b")[par])
            else:
                src = dr3(w1bf_d[e, :, q * 1024:(q + 1) * 1024])
                ydma(sb3(dst[:, :], KD), src, [DR(w1bf_d, e * 4 + q, e * 4 + q + 1)], wreg, sem=bsem)
            return par

        def load_w2_chunk(t, e, n, q):
            par = w2_chunk_idx[0] % 2
            w2_chunk_idx[0] += 1
            dst = w2b[par]
            wreg = [R(dst, 0, 128, 0, 8 * 512)]
            ci = e * 8 + n * 4 + q
            bsem = ("w2a", "w2b")[par]
            if t == 0:
                src = dr3(w2_d[e, q * 1024:(q + 1) * 1024, n * 512:(n + 1) * 512])
                gdma(sb3(dst[:, :], 8), src, [DR(w2_d, ci, ci + 1)], wreg,
                     sem=("c2a", "c2b")[par])
                sdst = dr3(w2bf_d[e, q * 1024:(q + 1) * 1024, n * 512:(n + 1) * 512])
                ydma(sdst, sb3(dst[:, :], 8), wreg, [DR(w2bf_d, ci, ci + 1)],
                     sem=("s2a", "s2b")[par])
            else:
                src = dr3(w2bf_d[e, q * 1024:(q + 1) * 1024, n * 512:(n + 1) * 512])
                ydma(sb3(dst[:, :], 8), src, [DR(w2bf_d, ci, ci + 1)], wreg, sem=bsem)
            return par

        p1banks = [pb[0], pb[1], py[2], py[3]]

        def expert_p1(t, e):
            def p1_mm(f, par):
                bank = p1banks[f % len(p1banks)]

                def mm1(tt, par=par, fl=f % 8, bank=bank):
                    ins = None
                    for k in range(KD):
                        ins = tt.matmul(bank[:, 0:512],
                                        w1b[par][:, k * 1024 + fl * 128: k * 1024 + (fl + 1) * 128],
                                        lnT[:, k * TT:(k + 1) * TT],
                                        start=(k == 0), stop=(k == KD - 1))
                    return ins
                sc.op("t", mm1,
                      reads=[R(w1b[par], 0, 128, 0, KD * 1024), R(lnT, 0, 128, 0, KD * TT)],
                      writes=[R(bank, 0, 128, 0, 512)])

            def p1_sX(f):
                bank = p1banks[f % len(p1banks)]
                p = f % 2
                xs = slice(p * 512, (p + 1) * 512)
                xreg = R(xg, 0, 128, p * 512, (p + 1) * 512)
                bcol = f * E + e
                sc.op("s", lambda s, xs=xs, bank=bank, bcol=bcol: s.activation(
                    xg[:, xs], bank[:, 0:512], AF.Identity, bias=b1col[:, bcol:bcol + 1], scale=1.0),
                    reads=[R(bank, 0, 128, 0, 512), R(b1col, 0, 128, bcol, bcol + 1)],
                    writes=[xreg])

            def p1_vUV(f):
                p = f % 2
                xs = slice(p * 512, (p + 1) * 512)
                xreg = R(xg, 0, 128, p * 512, (p + 1) * 512)
                ureg = R(ug, 0, 128, p * 512, (p + 1) * 512)
                vreg = R(vg, 0, 128, p * 512, (p + 1) * 512)
                sc.op("v", lambda v, xs=xs: v.tensor_tensor(
                    ug[:, xs], xg[:, xs], xg[:, xs], ALU.mult),
                    reads=[xreg], writes=[ureg])
                sc.op("v", lambda v, xs=xs: v.tensor_scalar(
                    ug[:, xs], ug[:, xs], GC1, GC2, ALU.mult, ALU.add),
                    reads=[ureg], writes=[ureg])
                sc.op("v", lambda v, xs=xs: v.tensor_tensor(
                    vg[:, xs], ug[:, xs], xg[:, xs], ALU.mult),
                    reads=[ureg, xreg], writes=[vreg])

            def p1_sC(f):
                p = f % 2
                xs = slice(p * 512, (p + 1) * 512)
                vreg = R(vg, 0, 128, p * 512, (p + 1) * 512)
                sc.op("s", lambda s, xs=xs: s.activation(vg[:, xs], vg[:, xs], AF.Sigmoid),
                      reads=[vreg], writes=[vreg])

            def p1_gD(f):
                p = f % 2
                xs = slice(p * 512, (p + 1) * 512)
                sc.op("v", lambda v, xs=xs, f=f: v.tensor_tensor(
                    hT[:, f * TT:(f + 1) * TT], vg[:, xs], xg[:, xs], ALU.mult),
                    reads=[R(vg, 0, 128, p * 512, (p + 1) * 512),
                           R(xg, 0, 128, p * 512, (p + 1) * 512)],
                    writes=[R(hT, 0, 128, f * TT, (f + 1) * TT)])

            for q in range(4):
                par = load_w1_chunk(t, e, q)
                for fl in range(8):
                    f = q * 8 + fl
                    p1_mm(f, par)
                    p1_sX(f)
                    if f >= 1:
                        p1_sC(f - 1)
                    p1_vUV(f)
                    if f >= 1:
                        p1_gD(f - 1)
            p1_sC(NF - 1)
            p1_gD(NF - 1)

        def expert_p2(t, e):
            for n in range(2):
                for q in range(4):
                    par = load_w2_chunk(t, e, n, q)
                    for i in range(NSUB):

                        def mm2(tt, par=par, q=q, i=i, n=n):
                            ins = None
                            for j in range(8):
                                f = q * 8 + j
                                ins = tt.matmul(
                                    py[i][:, 0:512],
                                    hT[:, f * TT + i * 128: f * TT + (i + 1) * 128],
                                    w2b[par][:, j * 512:(j + 1) * 512],
                                    start=(q == 0 and j == 0), stop=(q == 3 and j == 7),
                                    skip_group_check=True)
                            return ins
                        sc.op("t", mm2,
                              reads=[R(hT, 0, 128, q * 8 * TT, (q + 1) * 8 * TT),
                                     R(w2b[par], 0, 128, 0, 8 * 512)],
                              writes=[R(py[i], 0, 128, 0, 512)])
                        if q == 3:
                            ccol = i * E + e
                            d0 = i * D + n * 512
                            sc.op("v", lambda v, i=i, ccol=ccol, d0=d0: v.scalar_tensor_tensor(
                                acc_sb[:, d0:d0 + 512], py[i][:, 0:512], comb[:, ccol:ccol + 1],
                                acc_sb[:, d0:d0 + 512], ALU.mult, ALU.add),
                                reads=[R(py[i], 0, 128, 0, 512), R(comb, 0, 128, ccol, ccol + 1),
                                       R(acc_sb, 0, 128, d0, d0 + 512)],
                                writes=[R(acc_sb, 0, 128, d0, d0 + 512)])

        def ffn_sub(t, i, xbuf):
            bnk = pb[i % 2]

            def rmm(tt, i=i, bnk=bnk):
                ins = None
                for k in range(KD):
                    ins = tt.matmul(bnk[0:16, 0:128],
                                    a_bf[:, k * 16:(k + 1) * 16],
                                    lnT[:, k * TT + i * 128: k * TT + (i + 1) * 128],
                                    start=(k == 0), stop=(k == KD - 1))
                return ins
            sc.op("t", rmm,
                  reads=[R(a_bf, 0, 128, 0, KD * 16)]
                        + [R(lnT, 0, 128, k * TT + i * 128, k * TT + (i + 1) * 128) for k in range(KD)],
                  writes=[R(bnk, 0, 128, 0, 512)])
            sc.op("s", lambda s, i=i, bnk=bnk: s.activation(
                rT_sb[0:16, i * 128:(i + 1) * 128], bnk[0:16, 0:128],
                AF.Identity, bias=0.0, scale=1.0),
                reads=[R(bnk, 0, 128, 0, 512)],
                writes=[R(rT_sb, 0, 16, i * 128, (i + 1) * 128)])
            for n in range(2):
                bank = py[(2 * i + n) % 4]

                def fmm(tt, i=i, n=n, bank=bank):
                    tt.matmul(bank[:, 0:512], onesb[0:1, 0:128],
                              fbrow[0:1, n * 512:(n + 1) * 512], start=True, stop=False)
                    for k in range(KD):
                        tt.matmul(bank[:, 0:512],
                                  lnT[:, k * TT + i * 128: k * TT + (i + 1) * 128],
                                  wqb[:, k * 1024 + n * 512: k * 1024 + (n + 1) * 512],
                                  start=False, stop=False)
                    return tt.matmul(bank[:, 0:512], rT_sb[0:16, i * 128:(i + 1) * 128],
                                     bb_bf[0:16, n * 512:(n + 1) * 512], start=False, stop=True)
                sc.op("t", fmm,
                      reads=[R(onesb, 0, 1, 0, 128), R(fbrow, 0, 1, n * 512, (n + 1) * 512),
                             R(wqb, 0, 128, 0, KD * 1024),
                             R(rT_sb, 0, 16, i * 128, (i + 1) * 128),
                             R(bb_bf, 0, 16, n * 512, (n + 1) * 512)]
                            + [R(lnT, 0, 128, k * TT + i * 128, k * TT + (i + 1) * 128) for k in range(KD)],
                      writes=[R(bank, 0, 128, 0, 512)])
                d0 = i * D + n * 512
                sc.op("v", lambda v, bank=bank, d0=d0, xbuf=xbuf: v.tensor_tensor(
                    out_sb[:, d0:d0 + 512], bank[:, 0:512], xbuf[:, d0:d0 + 512], ALU.add),
                    reads=[R(bank, 0, 128, 0, 512), R(xbuf, 0, 128, d0, d0 + 512)],
                    writes=[R(out_sb, 0, 128, d0, d0 + 512)])

        def load_x(t):
            xbuf = xb[t % 2]
            gdma(sb3(xbuf[:, :], NSUB), dr3(x_d[t * TT:(t + 1) * TT, :]),
                 [DR(x_d, t, t + 1)], [R(xbuf, 0, 128, 0, NSUB * D)])

        load_x(0)
        for i in range(NSUB):
            layernorm(i, xb[0])
        preamble_b1()
        preamble_quant()
        for t in range(nt):
            x_sb = xb[t % 2]
            if do_router:
                for i in range(NSUB):
                    router(i)
            else:
                for idx in range(8):
                    i, n = idx // 2, idx % 2
                    sc.op("v", lambda v, i=i, n=n: v.memset(
                        acc_sb[:, i * D + n * 512: i * D + (n + 1) * 512], 0.0),
                        reads=[], writes=[R(acc_sb, 0, 128, i * D + n * 512, i * D + (n + 1) * 512)])
                sc.op("v", lambda v: v.memset(comb[:, :], 0.25),
                      reads=[], writes=[R(comb, 0, 128, 0, NSUB * E)])
                sc.op("v", lambda v: v.tensor_copy(comb_bf[:, :], comb[:, :]),
                      reads=[R(comb, 0, 128, 0, NSUB * E)], writes=[R(comb_bf, 0, 128, 0, NSUB * E)])
            for e in range(ne):
                expert_p1(t, e)
                if e == 0:
                    for i in range(NSUB):
                        router_fin(i)
                    b2_init()
                expert_p2(t, e)
            for i in range(NSUB):
                sc.op("v", lambda v, i=i, x_sb=x_sb: v.tensor_tensor(
                    x_sb[:, i * D:(i + 1) * D], x_sb[:, i * D:(i + 1) * D],
                    acc_sb[:, i * D:(i + 1) * D], ALU.add),
                    reads=[R(x_sb, 0, 128, i * D, (i + 1) * D), R(acc_sb, 0, 128, i * D, (i + 1) * D)],
                    writes=[R(x_sb, 0, 128, i * D, (i + 1) * D)])
                if do_ffn:
                    layernorm(i, x_sb)
                    ffn_sub(t, i, x_sb)
            if t + 1 < nt:
                load_x(t + 1)
                for i in range(NSUB):
                    layernorm(i, xb[(t + 1) % 2])
            if do_debug and t == 0:
                gdma(dbg["dbg_x1"], x_sb[:, :], [R(x_sb, 0, 128, 0, 4096)], [], sem="dbg")
            if not do_ffn:
                for idx in range(8):
                    i, n = idx // 2, idx % 2
                    d0 = i * D + n * 512
                    sc.op("v", lambda v, d0=d0: v.tensor_copy(
                        out_sb[:, d0:d0 + 512], x_sb[:, d0:d0 + 512]),
                        reads=[R(x_sb, 0, 128, d0, d0 + 512)],
                        writes=[R(out_sb, 0, 128, d0, d0 + 512)])
            gdma(dr3(y_d[t * TT:(t + 1) * TT, :]), sb3(out_sb[:, :], NSUB),
                 [R(out_sb, 0, 128, 0, NSUB * D)], [DR(y_d, t, t + 1)], sem="gy")

        if do_debug:
            if do_router:
                gdma(dbg["dbg_comb"], comb[:, :], [R(comb, 0, 128, 0, 32)], [], sem="dbg")
                gdma(dbg["dbg_e"], e_sb[:, :], [R(e_sb, 0, 128, 0, 32)], [], sem="dbg")
                sc.op("v", lambda v: v.tensor_copy(xg[0:8, 0:512], combT[0:8, :]),
                      reads=[R(combT, 0, 8, 0, 512)], writes=[R(xg, 0, 8, 0, 512)])
                gdma(dbg["dbg_combT"], xg[0:8, 0:512], [R(xg, 0, 8, 0, 512)], [], sem="dbg")
                gdma(dbg["dbg_rsc"], rsc[:, :], [R(rsc, 0, 128, 0, 16)], [], sem="dbg")
            gdma(dbg["dbg_acc"], acc_sb[:, :], [R(acc_sb, 0, 128, 0, 4096)], [], sem="dbg")
            sc.op("v", lambda v: v.tensor_copy(xg[:, 0:512], rT_sb[0:16, :].rearrange("p (a c) -> p a c", a=1)[:, 0, :]) if False else v.tensor_copy(ug[0:16, 0:512], rT_sb[0:16, :]),
                  reads=[R(rT_sb, 0, 16, 0, TT)], writes=[R(ug, 0, 16, 0, 512)])
            gdma(dbg["dbg_rT"], ug[0:16, 0:512], [R(ug, 0, 16, 0, 512)], [], sem="dbg")
            sc.op("v", lambda v: v.tensor_copy(vg[:, 0:512], wqb[:, 0:512]),
                  reads=[R(wqb, 0, 128, 0, 512)], writes=[R(vg, 0, 128, 0, 512)])
            gdma(dbg["dbg_wq"][:, 0:512], vg[:, 0:512], [R(vg, 0, 128, 0, 512)], [], sem="dbg")
            gdma(dbg["dbg_qsc"], qsc[:, :], [R(qsc, 0, 128, 0, 2)], [], sem="dbg")
            gdma(dbg["dbg_lnT"], lnT[:, :].rearrange("p (a c) -> p a c", a=KD),
                 [R(lnT, 0, 128, 0, KD * TT)], [], sem="dbg")
            if ne > 0:
                gdma(dbg["dbg_hT"], hT[:, 0:2048].rearrange("p (a c) -> p a c", a=4),
                     [R(hT, 0, 128, 0, 2048)], [], sem="dbg")

        for smm in Sched.DMA_SEMS:
            if sc.cnt[smm] > 0:
                sc.raw_wait("g", smm, sc.cnt[smm])

        with nc.Block() as blk:
            @blk.tensor
            def _(engine):
                sc.replay("t", engine, sems)

            @blk.vector
            def _(engine):
                sc.replay("v", engine, sems)

            @blk.scalar
            def _(engine):
                sc.replay("s", engine, sems)

            @blk.gpsimd
            def _(engine):
                sc.replay("g", engine, sems)

            @blk.sync
            def _(engine):
                sc.replay("y", engine, sems)

    return nc


_NC_CACHE = {}


def kernel(L0_attn_hidden_states, L0_attn_gate_w, L0_attn_w1, L0_attn_b1,
           L0_attn_w2, L0_attn_b2, L0_ffn_kernel, L0_ffn_bias, L0_ffn_a, L0_ffn_b):
    x = np.ascontiguousarray(np.asarray(L0_attn_hidden_states, dtype=np.float32))
    xf = x.reshape(B * S, D)
    gate = np.ascontiguousarray(np.asarray(L0_attn_gate_w, dtype=np.float32))
    w1 = np.ascontiguousarray(np.asarray(L0_attn_w1, dtype=np.float32))
    b1 = np.ascontiguousarray(np.asarray(L0_attn_b1, dtype=np.float32))
    w2 = np.ascontiguousarray(np.asarray(L0_attn_w2, dtype=np.float32))
    b2 = np.ascontiguousarray(np.asarray(L0_attn_b2, dtype=np.float32))
    fk = np.ascontiguousarray(np.asarray(L0_ffn_kernel, dtype=np.float32))
    fb = np.ascontiguousarray(np.asarray(L0_ffn_bias, dtype=np.float32)).reshape(1, D)
    fa = np.ascontiguousarray(np.asarray(L0_ffn_a, dtype=np.float32))
    fbb = np.ascontiguousarray(np.asarray(L0_ffn_b, dtype=np.float32))
    identb = np.eye(128, dtype=ml_dtypes.bfloat16)
    identf = np.eye(128, dtype=np.float32)
    consts = np.tile(np.array([[1e-5, -0.5, GC1, GC2]], dtype=np.float32), (128, 1))
    onesf = np.ones((1, 512), dtype=np.float32)

    if "nc" not in _NC_CACHE:
        _NC_CACHE["nc"] = build_program()
    nc = _NC_CACHE["nc"]

    in_maps = []
    for c in range(NCORES):
        in_maps.append({
            "x": np.ascontiguousarray(xf[c * TC:(c + 1) * TC]),
            "gate": gate, "w1": w1, "b1": b1, "w2": w2, "b2": b2,
            "fk": fk, "fb": fb, "fa": fa, "fbb": fbb,
            "identb": identb, "identf": identf, "consts": consts, "onesf": onesf,
        })
    res = run_bass_kernel_spmd(nc, in_maps, core_ids=list(range(NCORES)))
    out = np.concatenate([res.results[c]["y"] for c in range(NCORES)], axis=0)
    return out.reshape(B, S, D)


# version 21
# speedup vs baseline: 1.1762x; 1.1762x over previous
import sys

for _p in ("/opt/trn_rl_repo",):
    if _p not in sys.path:
        sys.path.insert(0, _p)

import numpy as np
import ml_dtypes
from contextlib import ExitStack

import concourse.bass as bass
import concourse.mybir as mybir
import concourse.bass_utils as _bass_utils
from concourse.bass_utils import run_bass_kernel_spmd

_orig_bvo = _bass_utils.bir_verify_and_optimise


def _bvo_ldw(*args, **kwargs):
    import subprocess as _sp
    orig_run = _bass_utils.run_command

    def patched_run(cmd, **kw):
        cmd = [c.replace("--enable-ldw-opt=false", "--enable-ldw-opt=true")
               if isinstance(c, str) else c for c in cmd]
        return orig_run(cmd, **kw)

    _bass_utils.run_command = patched_run
    try:
        return _orig_bvo(*args, **kwargs)
    finally:
        _bass_utils.run_command = orig_run


_bass_utils.bir_verify_and_optimise = _bvo_ldw

F32 = mybir.dt.float32
BF16 = mybir.dt.bfloat16
I32 = mybir.dt.int32
AF = mybir.ActivationFunctionType
ALU = mybir.AluOpType
AX = mybir.AxisListType.X

NCORES = 8
B, S, D, E, FF = 8, 4096, 1024, 8, 4096
TC = B * S // NCORES
TT = 512
NT = TC // TT
NSUB = TT // 128
KD = D // 128
NF = FF // 128
GC1 = 0.071354816
GC2 = 1.5957691216


class Sched:

    ENGINES = ("t", "v", "s", "g", "y")
    SELF_EDGES = {"t": False, "v": True, "s": True, "g": True, "y": True}
    SELF_SKIP_MARGIN = {"v": 16, "s": 16}
    INC = {"t": 1, "v": 1, "s": 1, "g": 16, "y": 16}
    DMA_SEMS = ("gc", "gx", "gy", "kq", "b1a", "b1b", "w1a", "w1b", "w2a", "w2b",
                "s1a", "s1b", "s2a", "s2b", "c1a", "c1b", "c2a", "c2b",
                "gxa", "gxb", "dbg")

    def __init__(self):
        self.prog = {e: [] for e in self.ENGINES}
        self.cnt = {e: 0 for e in self.ENGINES}
        for smm in self.DMA_SEMS:
            self.cnt[smm] = 0
        self.vc = {e: {} for e in self.ENGINES}
        self.snap = {e: {} for e in self.ENGINES}
        for smm in self.DMA_SEMS:
            self.snap[smm] = {}
        self.writers = {}
        self.readers = {}

    @staticmethod
    def _overlap(a, b):
        return a[0] < b[1] and b[0] < a[1] and a[2] < b[3] and b[2] < a[3]

    @staticmethod
    def _covers(a, b):
        return a[0] <= b[0] and a[1] >= b[1] and a[2] <= b[2] and a[3] >= b[3]

    def _need_from(self, regions, want_writers, want_readers, need):
        for r in regions:
            key, ival = r[0], r[1:]
            if want_writers:
                for w in self.writers.get(key, ()):
                    if self._overlap(ival, w[:4]):
                        e, c = w[4], w[5]
                        if need.get(e, 0) < c:
                            need[e] = c
            if want_readers:
                for (rival, e), c in self.readers.get(key, {}).items():
                    if self._overlap(ival, rival):
                        if need.get(e, 0) < c:
                            need[e] = c

    def _snap_at(self, sem, cnt):
        sn = self.snap[sem]
        best = None
        bk = -1
        for k, v in sn.items():
            if bk < k <= cnt:
                bk, best = k, v
        return best

    def op(self, eng, emit, reads=(), writes=(), inc=None, sem=None, defer_commit=False):
        mysem = sem if sem is not None else eng
        need = {}
        self._need_from(reads, True, False, need)
        self._need_from(writes, True, True, need)
        vc = self.vc[eng]
        margin = self.SELF_SKIP_MARGIN.get(eng)
        waits = []
        for dsem, dcnt in sorted(need.items()):
            if dsem == mysem and mysem == eng:
                if not self.SELF_EDGES[eng]:
                    continue
                if margin is not None and self.cnt[eng] - dcnt >= margin:
                    continue
            if vc.get(dsem, 0) >= dcnt:
                continue
            waits.append((dsem, dcnt))
        for dsem, dcnt in waits:
            if vc.get(dsem, 0) < dcnt:
                vc[dsem] = dcnt
            sn = self._snap_at(dsem, dcnt)
            if sn:
                for k, v in sn.items():
                    if vc.get(k, 0) < v:
                        vc[k] = v
        step = self.INC[eng] if inc is None else inc
        self.cnt[mysem] += step
        newcnt = self.cnt[mysem]
        self.prog[eng].append((tuple(waits), emit, mysem, step))
        sn = dict(vc)
        sn[mysem] = newcnt
        self.snap[mysem][newcnt] = sn
        if not defer_commit:
            self.commit(mysem, newcnt, reads, writes)
        return newcnt

    def commit(self, mysem, newcnt, reads, writes):
        for r in reads:
            key, ival = r[0], r[1:]
            rd = self.readers.setdefault(key, {})
            k2 = (ival, mysem)
            if rd.get(k2, 0) < newcnt:
                rd[k2] = newcnt
        for r in writes:
            key, ival = r[0], r[1:]
            ws = self.writers.setdefault(key, [])
            ws[:] = [w for w in ws if not self._covers(ival, w[:4])]
            ws.append((*ival, mysem, newcnt))
            rd = self.readers.get(key)
            if rd:
                for k2 in [k for k in rd if self._covers(ival, k[0])]:
                    del rd[k2]

    def raw_wait(self, eng, dsem, dcnt):
        self.prog[eng].append((((dsem, dcnt),), None, eng, 0))

    def replay(self, eng, engine, sems):
        for waits, emit, mysem, step in self.prog[eng]:
            for dsem, dcnt in waits:
                engine.wait_ge(sems[dsem], dcnt)
            if emit is not None:
                ins = emit(engine)
                if step:
                    ins.then_inc(sems[mysem], step)


def R(t, p0, p1, f0, f1):
    return (id(t), p0, p1, f0, f1)


def dr3(dram_slice, p=128):
    return dram_slice.rearrange("(a p) c -> p a c", p=p)


def sb3(sbuf_ap, a):
    return sbuf_ap.rearrange("p (a c) -> p a c", a=a)


def build_program(nt=NT, ne=E, do_router=True, do_ffn=True, do_quant=True, do_b1prep=True, do_smalls=True, do_debug=False):
    nc = bass.Bass("TRN2", target_bir_lowering=False)

    x_d = nc.dram_tensor("x", [TC, D], F32, kind="ExternalInput").ap()
    gate_d = nc.dram_tensor("gate", [D, E], F32, kind="ExternalInput").ap()
    w1_d = nc.dram_tensor("w1", [E, D, FF], F32, kind="ExternalInput").ap()
    b1_d = nc.dram_tensor("b1", [E, FF], F32, kind="ExternalInput").ap()
    w2_d = nc.dram_tensor("w2", [E, FF, D], F32, kind="ExternalInput").ap()
    b2_d = nc.dram_tensor("b2", [E, D], F32, kind="ExternalInput").ap()
    fk_d = nc.dram_tensor("fk", [D, D], F32, kind="ExternalInput").ap()
    fb_d = nc.dram_tensor("fb", [1, D], F32, kind="ExternalInput").ap()
    fa_d = nc.dram_tensor("fa", [D, 16], F32, kind="ExternalInput").ap()
    fbb_d = nc.dram_tensor("fbb", [16, D], F32, kind="ExternalInput").ap()
    identb_d = nc.dram_tensor("identb", [128, 128], BF16, kind="ExternalInput").ap()
    identf_d = nc.dram_tensor("identf", [128, 128], F32, kind="ExternalInput").ap()
    consts_d = nc.dram_tensor("consts", [128, 4], F32, kind="ExternalInput").ap()
    onesf_d = nc.dram_tensor("onesf", [1, 512], F32, kind="ExternalInput").ap()
    y_d = nc.dram_tensor("y", [TC, D], F32, kind="ExternalOutput").ap()
    dbg = {}
    if do_debug:
        for nm, shape, dt in [("dbg_comb", [128, 32], F32), ("dbg_e", [128, 32], F32),
                              ("dbg_combT", [8, 512], F32), ("dbg_acc", [128, 4096], F32),
                              ("dbg_x1", [128, 4096], F32), ("dbg_lnT", [128, 4096], F32),
                              ("dbg_hT", [128, 2048], F32), ("dbg_rsc", [128, 16], F32),
                              ("dbg_wq", [128, 8192], F32), ("dbg_qsc", [128, 2], F32),
                              ("dbg_rT", [16, 512], F32)]:
            dbg[nm] = nc.dram_tensor(nm, shape, dt, kind="ExternalOutput").ap()
    w1bf_d = nc.dram_tensor("w1bf", [E, D, FF], BF16, kind="Internal").ap()
    w2bf_d = nc.dram_tensor("w2bf", [E, FF, D], BF16, kind="Internal").ap()

    es = ExitStack()
    with es:
        sb = lambda name, shape, dt: es.enter_context(nc.sbuf_tensor(name, shape, dt))
        ps = lambda name, shape, dt: es.enter_context(nc.psum_tensor(name, shape, dt))

        xb = [sb("x_sbA", [128, NSUB * D], F32), sb("x_sbB", [128, NSUB * D], F32)]
        out_sb = sb("out_sb", [128, NSUB * D], F32)
        acc_sb = sb("acc_sb", [128, NSUB * D], F32)
        lnst = sb("lnst", [128, NSUB * D], BF16)
        lnT = sb("lnT", [128, KD * TT], BF16)
        hT = sb("hT", [128, NF * TT], BF16)
        w1b = [sb(f"w1b{i}", [128, KD * 1024], BF16) for i in range(2)]
        w2b = [sb(f"w2b{i}", [128, 8 * 512], BF16) for i in range(2)]
        wqb = sb("wqb", [128, KD * 1024], BF16)
        kstage = sb("kstage", [128, 2048], F32)
        xg = sb("xg", [128, 2 * 512], BF16)
        ug = sb("ug", [128, 2 * 512], BF16)
        vg = sb("vg", [128, 2 * 512], BF16)
        b1st = sb("b1st", [8, 2 * 128], F32)
        b1col = sb("b1col", [128, NF * E], F32)
        gate_bf = sb("gate_bf", [128, KD * E], BF16)
        b2_bf = sb("b2_bf", [8, D], BF16)
        a_bf = sb("a_bf", [128, KD * 16], BF16)
        bb_bf = sb("bb_bf", [16, D], BF16)
        fbrow = sb("fbrow", [1, D], BF16)
        onesb = sb("onesb", [1, 512], BF16)
        identb_sb = sb("identb_sb", [128, 128], BF16)
        identf_sb = sb("identf_sb", [128, 128], F32)
        consts_sb = sb("consts_sb", [128, 4], F32)
        stats = sb("stats", [128, 12], F32)
        mv = sb("mv", [128, 2], F32)
        lnsc = sb("lnsc", [128, 4], F32)
        e_sb = sb("e_sb", [128, NSUB * E], F32)
        rsc = sb("rsc", [128, NSUB * 4], F32)
        rsc2 = sb("rsc2", [128, NSUB * 4], F32)
        ris = sb("ris", [128, E], F32)
        rwo = sb("rwo", [128, E], F32)
        rkeep = sb("rkeep", [128, E], F32)
        comb = sb("comb", [128, NSUB * E], F32)
        comb_bf = sb("comb_bf", [128, NSUB * E], BF16)
        combT = sb("combT", [8, NSUB * 128], BF16)
        rT_sb = sb("rT_sb", [16, TT], BF16)
        qsc = sb("qsc", [128, 2], F32)
        qrow = sb("qrow", [1, 4], F32)
        qhi = sb("qhi", [1, 2], BF16)
        qhf = sb("qhf", [1, 2], F32)
        qdf = sb("qdf", [1, 2], F32)
        qlo = sb("qlo", [1, 2], BF16)
        cmax = sb("cmax", [128, 4], F32)
        qmax = sb("qmax", [128, 1], F32)

        pb = [ps(f"pb{i}", [128, 512], F32) for i in range(2)]
        py = [ps(f"py{i}", [128, 512], F32) for i in range(4)]
        ptrs = [ps("ptra", [128, 1024], BF16), ps("ptrb", [128, 1024], BF16)]

        sems = {}
        for nm in Sched.ENGINES + Sched.DMA_SEMS:
            sems[nm] = es.enter_context(nc.semaphore(f"sem_{nm}"))

        sc = Sched()

        def gdma(dst_ap, src_ap, reads, writes, sem="gx", defer_commit=False):
            return sc.op("g", lambda g, d=dst_ap, s=src_ap: g.dma_start(d, s),
                         reads=reads, writes=writes, sem=sem, defer_commit=defer_commit)

        def ydma(dst_ap, src_ap, reads, writes, sem="kq"):
            return sc.op("y", lambda y, d=dst_ap, s=src_ap: y.dma_start(d, s),
                         reads=reads, writes=writes, sem=sem)

        DR = lambda t, i0=0, i1=1: (("dram", id(t)), 0, 1, i0, i1)

        const_writes = []

        def cdma(dst_ap, src_ap, rd, wr):
            gdma(dst_ap, src_ap, rd, wr, sem="gc", defer_commit=True)
            const_writes.extend(wr)

        cdma(identb_sb[:, :], identb_d, [DR(identb_d)], [R(identb_sb, 0, 128, 0, 128)])
        cdma(identf_sb[:, :], identf_d, [DR(identf_d)], [R(identf_sb, 0, 128, 0, 128)])
        cdma(consts_sb[:, :], consts_d, [DR(consts_d)], [R(consts_sb, 0, 128, 0, 4)])
        cdma(onesb[:, :], onesf_d, [DR(onesf_d)], [R(onesb, 0, 1, 0, 512)])
        if do_smalls:
            cdma(sb3(gate_bf[:, :], KD), dr3(gate_d), [DR(gate_d)], [R(gate_bf, 0, 128, 0, KD * E)])
            cdma(b2_bf[:, :], b2_d, [DR(b2_d)], [R(b2_bf, 0, 8, 0, D)])
            cdma(sb3(a_bf[:, :], KD), dr3(fa_d), [DR(fa_d)], [R(a_bf, 0, 128, 0, KD * 16)])
            cdma(bb_bf[:, :], fbb_d, [DR(fbb_d)], [R(bb_bf, 0, 16, 0, D)])
            cdma(fbrow[:, :], fb_d, [DR(fb_d)], [R(fbrow, 0, 1, 0, D)])
        sc.commit("gc", sc.cnt["gc"], [], const_writes)

        def preamble_b1():
            for f in range(NF if do_b1prep else 0):
                par = f % 2
                gdma(b1st[:, par * 128:(par + 1) * 128],
                     b1_d[:, f * 128:(f + 1) * 128],
                     [DR(b1_d)], [R(b1st, 0, 8, par * 128, (par + 1) * 128)],
                     sem=("b1a", "b1b")[par])
                slot = f % 8
                sc.op("t", lambda t, par=par, slot=slot: t.transpose(
                    pb[0][:, slot * 8:(slot + 1) * 8],
                    b1st[0:8, par * 128:(par + 1) * 128],
                    identf_sb[0:8, 0:8]),
                    reads=[R(b1st, 0, 8, par * 128, (par + 1) * 128),
                           R(identf_sb, 0, 8, 0, 8)],
                    writes=[R(pb[0], 0, 128, slot * 8, (slot + 1) * 8)])
                if slot == 7:
                    grp = f // 8
                    sc.op("s", lambda s, grp=grp: s.activation(
                        b1col[:, grp * 64:(grp + 1) * 64], pb[0][:, 0:64],
                        AF.Identity, bias=0.0, scale=1.0),
                        reads=[R(pb[0], 0, 128, 0, 64)],
                        writes=[R(b1col, 0, 128, grp * 64, (grp + 1) * 64)])

        def preamble_quant():
            def load_fk_quarter(h):
                ydma(sb3(kstage[:, :], 2), dr3(fk_d[h * 256:(h + 1) * 256, :]),
                     [DR(fk_d)], [R(kstage, 0, 128, 0, 2048)])

            for h in range(4 if do_quant else 0):
                load_fk_quarter(h)
                sc.op("v", lambda v, h=h: v.reduce_max(
                    cmax[:, h:h + 1], kstage[:, :], axis=AX, apply_absolute_value=True),
                    reads=[R(kstage, 0, 128, 0, 2048)],
                    writes=[R(cmax, 0, 128, h, h + 1)])
            if not do_quant:
                sc.op("v", lambda v: v.memset(wqb[:, :], 0.0), reads=[], writes=[R(wqb, 0, 128, 0, KD * 1024)])
            if do_quant:
              sc.op("v", lambda v: v.tensor_tensor(qmax[:, 0:1], cmax[:, 0:1], cmax[:, 1:2], ALU.max),
                    reads=[R(cmax, 0, 128, 0, 2)], writes=[R(qmax, 0, 128, 0, 1)])
              sc.op("v", lambda v: v.tensor_tensor(cmax[:, 2:3], cmax[:, 2:3], cmax[:, 3:4], ALU.max),
                    reads=[R(cmax, 0, 128, 2, 4)], writes=[R(cmax, 0, 128, 2, 3)])
              sc.op("v", lambda v: v.tensor_tensor(qmax[:, 0:1], qmax[:, 0:1], cmax[:, 2:3], ALU.max),
                    reads=[R(qmax, 0, 128, 0, 1), R(cmax, 0, 128, 2, 3)],
                    writes=[R(qmax, 0, 128, 0, 1)])
              sc.op("t", lambda t: t.transpose(pb[0][0:1, 0:128], qmax[:, 0:1], identf_sb[:, :]),
                    reads=[R(qmax, 0, 128, 0, 1), R(identf_sb, 0, 128, 0, 128)],
                    writes=[R(pb[0], 0, 1, 0, 128)])
              sc.op("v", lambda v: v.reduce_max(qrow[0:1, 0:1], pb[0][0:1, 0:128], axis=AX),
                    reads=[R(pb[0], 0, 1, 0, 128)], writes=[R(qrow, 0, 1, 0, 1)])
              sc.op("v", lambda v: v.reciprocal(qrow[0:1, 1:2], qrow[0:1, 0:1]),
                    reads=[R(qrow, 0, 1, 0, 1)], writes=[R(qrow, 0, 1, 1, 2)])
              sc.op("v", lambda v: v.tensor_scalar(qrow[0:1, 2:3], qrow[0:1, 1:2], 7.0, None, ALU.mult),
                    reads=[R(qrow, 0, 1, 1, 2)], writes=[R(qrow, 0, 1, 2, 3)])
              sc.op("v", lambda v: v.tensor_scalar(qrow[0:1, 3:4], qrow[0:1, 0:1], 1.0 / 7.0, None, ALU.mult),
                    reads=[R(qrow, 0, 1, 0, 1)], writes=[R(qrow, 0, 1, 3, 4)])
              sc.op("v", lambda v: v.tensor_copy(qhi[0:1, 0:2], qrow[0:1, 2:4]),
                    reads=[R(qrow, 0, 1, 2, 4)], writes=[R(qhi, 0, 1, 0, 2)])
              sc.op("v", lambda v: v.tensor_copy(qhf[0:1, 0:2], qhi[0:1, 0:2]),
                    reads=[R(qhi, 0, 1, 0, 2)], writes=[R(qhf, 0, 1, 0, 2)])
              sc.op("v", lambda v: v.tensor_tensor(qdf[0:1, 0:2], qrow[0:1, 2:4], qhf[0:1, 0:2], ALU.subtract),
                    reads=[R(qrow, 0, 1, 2, 4), R(qhf, 0, 1, 0, 2)], writes=[R(qdf, 0, 1, 0, 2)])
              sc.op("v", lambda v: v.tensor_copy(qlo[0:1, 0:2], qdf[0:1, 0:2]),
                    reads=[R(qdf, 0, 1, 0, 2)], writes=[R(qlo, 0, 1, 0, 2)])

              def qbcast(t):
                  t.matmul(pb[1][:, 0:2], onesb[0:1, 0:128], qhi[0:1, 0:2], start=True, stop=False)
                  return t.matmul(pb[1][:, 0:2], onesb[0:1, 0:128], qlo[0:1, 0:2], start=False, stop=True)

              sc.op("t", qbcast,
                    reads=[R(onesb, 0, 1, 0, 128), R(qhi, 0, 1, 0, 2), R(qlo, 0, 1, 0, 2)],
                    writes=[R(pb[1], 0, 128, 0, 2)])
              sc.op("s", lambda s: s.activation(qsc[:, :], pb[1][:, 0:2], AF.Identity, bias=0.0, scale=1.0),
                    reads=[R(pb[1], 0, 128, 0, 2)], writes=[R(qsc, 0, 128, 0, 2)])

              for h in range(4):
                  load_fk_quarter(h)
                  kr = [R(kstage, 0, 128, 0, 2048)]
                  kw = kr
                  sc.op("v", lambda v: v.tensor_scalar(kstage[:, :], kstage[:, :], qsc[:, 0:1], None, ALU.mult),
                        reads=kr + [R(qsc, 0, 128, 0, 1)], writes=kw)
                  sc.op("v", lambda v: v.tensor_scalar(kstage[:, :], kstage[:, :], 12582912.0, None, ALU.add),
                        reads=kr, writes=kw)
                  sc.op("v", lambda v: v.tensor_scalar(kstage[:, :], kstage[:, :], -12582912.0, None, ALU.add),
                        reads=kr, writes=kw)
                  sc.op("v", lambda v: v.tensor_scalar(kstage[:, :], kstage[:, :], 7.0, None, ALU.min),
                        reads=kr, writes=kw)
                  sc.op("v", lambda v: v.tensor_scalar(kstage[:, :], kstage[:, :], -8.0, None, ALU.max),
                        reads=kr, writes=kw)
                  sc.op("v", lambda v, h=h: v.tensor_scalar(
                      wqb[:, h * 2048:(h + 1) * 2048], kstage[:, :], qsc[:, 1:2], None, ALU.mult),
                      reads=kr + [R(qsc, 0, 128, 1, 2)],
                      writes=[R(wqb, 0, 128, h * 2048, (h + 1) * 2048)])


        def layernorm(i, xbuf):
            xr = R(xbuf, 0, 128, i * D, (i + 1) * D)
            l0 = i * D
            sc.op("v", lambda v, i=i, xbuf=xbuf: v.bn_stats(stats[:, 0:6], xbuf[:, i * D:i * D + 512]),
                  reads=[xr], writes=[R(stats, 0, 128, 0, 6)])
            sc.op("v", lambda v, i=i, xbuf=xbuf: v.bn_stats(stats[:, 6:12], xbuf[:, i * D + 512:(i + 1) * D]),
                  reads=[xr], writes=[R(stats, 0, 128, 6, 12)])
            sc.op("v", lambda v: v.bn_aggr(mv[:, :], stats[:, :]),
                  reads=[R(stats, 0, 128, 0, 12)], writes=[R(mv, 0, 128, 0, 2)])
            sc.op("s", lambda s: s.activation(lnsc[:, 0:1], mv[:, 0:1], AF.Identity, bias=0.0, scale=1.0),
                  reads=[R(mv, 0, 128, 0, 1)], writes=[R(lnsc, 0, 128, 0, 1)])
            sc.op("s", lambda s: s.activation(lnsc[:, 1:2], mv[:, 1:2], AF.Ln, bias=consts_sb[:, 0:1], scale=1.0),
                  reads=[R(mv, 0, 128, 1, 2), R(consts_sb, 0, 128, 0, 1)],
                  writes=[R(lnsc, 0, 128, 1, 2)])
            sc.op("s", lambda s: s.activation(lnsc[:, 2:3], lnsc[:, 1:2], AF.Exp, bias=0.0, scale=consts_sb[:, 1:2]),
                  reads=[R(lnsc, 0, 128, 1, 2), R(consts_sb, 0, 128, 1, 2)],
                  writes=[R(lnsc, 0, 128, 2, 3)])
            sc.op("v", lambda v, i=i, xbuf=xbuf, l0=l0: v.tensor_scalar(
                lnst[:, l0:l0 + D], xbuf[:, i * D:(i + 1) * D], lnsc[:, 0:1], lnsc[:, 2:3],
                ALU.subtract, ALU.mult),
                reads=[xr, R(lnsc, 0, 128, 0, 1), R(lnsc, 0, 128, 2, 3)],
                writes=[R(lnst, 0, 128, l0, l0 + D)])
            for k in range(KD):
                pt = ptrs[k % 2]
                sc.op("t", lambda t, k=k, pt=pt, l0=l0: t.transpose(
                    pt[:, 0:128], lnst[:, l0 + k * 128:l0 + (k + 1) * 128], identb_sb[:, :]),
                    reads=[R(lnst, 0, 128, l0 + k * 128, l0 + (k + 1) * 128), R(identb_sb, 0, 128, 0, 128)],
                    writes=[R(pt, 0, 128, 0, 1024)])
                sc.op("s", lambda s, k=k, i=i, pt=pt: s.activation(
                    lnT[:, k * TT + i * 128: k * TT + (i + 1) * 128],
                    pt[:, 0:128], AF.Identity, bias=0.0, scale=1.0),
                    reads=[R(pt, 0, 128, 0, 1024)],
                    writes=[R(lnT, 0, 128, k * TT + i * 128, k * TT + (i + 1) * 128)])

        def router(i):
            def rmm(t, i=i):
                ins = None
                for k in range(KD):
                    ins = t.matmul(py[3][:, 0:8],
                                   lnT[:, k * TT + i * 128: k * TT + (i + 1) * 128],
                                   gate_bf[:, k * E:(k + 1) * E],
                                   start=(k == 0), stop=(k == KD - 1))
                return ins
            sc.op("t", rmm,
                  reads=[R(lnT, 0, 128, k * TT + i * 128, k * TT + (i + 1) * 128) for k in range(KD)]
                        + [R(gate_bf, 0, 128, 0, KD * E)],
                  writes=[R(py[3], 0, 128, 0, 512)])
            sc.op("v", lambda v: v.reduce_max(lnsc[:, 3:4], py[3][:, 0:8], axis=AX, negate=True),
                  reads=[R(py[3], 0, 128, 0, 512)], writes=[R(lnsc, 0, 128, 3, 4)])
            ei0, ei1 = i * E, (i + 1) * E
            sc.op("s", lambda s, ei0=ei0, ei1=ei1: s.activation(
                e_sb[:, ei0:ei1], py[3][:, 0:8], AF.Exp, bias=lnsc[:, 3:4], scale=1.0),
                reads=[R(py[3], 0, 128, 0, 512), R(lnsc, 0, 128, 3, 4)],
                writes=[R(e_sb, 0, 128, ei0, ei1)])
            c0 = i * 4
            sc.op("v", lambda v, ei0=ei0, ei1=ei1, c0=c0: v.reduce_max(
                rsc[:, c0:c0 + 1], e_sb[:, ei0:ei1], axis=AX),
                reads=[R(e_sb, 0, 128, ei0, ei1)], writes=[R(rsc, 0, 128, c0, c0 + 1)])
            sc.op("s", lambda s, c0=c0: s.activation(
                rsc2[:, c0:c0 + 1], rsc[:, c0:c0 + 1], AF.Identity, bias=0.0, scale=1.0),
                reads=[R(rsc, 0, 128, c0, c0 + 1)], writes=[R(rsc2, 0, 128, c0, c0 + 1)])
            sc.op("v", lambda v, ei0=ei0, ei1=ei1, c0=c0: v.tensor_scalar(
                ris[:, :], e_sb[:, ei0:ei1], rsc2[:, c0:c0 + 1], None, ALU.is_ge),
                reads=[R(e_sb, 0, 128, ei0, ei1), R(rsc2, 0, 128, c0, c0 + 1)],
                writes=[R(ris, 0, 128, 0, E)])
            sc.op("v", lambda v, ei0=ei0, ei1=ei1: v.scalar_tensor_tensor(
                rwo[:, :], ris[:, :], -2.0, e_sb[:, ei0:ei1], ALU.mult, ALU.add),
                reads=[R(ris, 0, 128, 0, E), R(e_sb, 0, 128, ei0, ei1)],
                writes=[R(rwo, 0, 128, 0, E)])
            sc.op("v", lambda v, c0=c0: v.reduce_max(rsc[:, c0 + 1:c0 + 2], rwo[:, :], axis=AX),
                  reads=[R(rwo, 0, 128, 0, E)], writes=[R(rsc, 0, 128, c0 + 1, c0 + 2)])
            sc.op("v", lambda v, c0=c0: v.tensor_tensor(
                rsc[:, c0 + 2:c0 + 3], rsc[:, c0:c0 + 1], rsc[:, c0 + 1:c0 + 2], ALU.add),
                reads=[R(rsc, 0, 128, c0, c0 + 2)], writes=[R(rsc, 0, 128, c0 + 2, c0 + 3)])
            sc.op("v", lambda v, c0=c0: v.reciprocal(rsc[:, c0 + 3:c0 + 4], rsc[:, c0 + 2:c0 + 3]),
                  reads=[R(rsc, 0, 128, c0 + 2, c0 + 3)], writes=[R(rsc, 0, 128, c0 + 3, c0 + 4)])
            sc.op("s", lambda s, c0=c0: s.activation(
                rsc2[:, c0 + 1:c0 + 4], rsc[:, c0 + 1:c0 + 4], AF.Identity, bias=0.0, scale=1.0),
                reads=[R(rsc, 0, 128, c0 + 1, c0 + 4)], writes=[R(rsc2, 0, 128, c0 + 1, c0 + 4)])
            sc.op("v", lambda v, ei0=ei0, ei1=ei1, c0=c0: v.tensor_scalar(
                rkeep[:, :], e_sb[:, ei0:ei1], rsc2[:, c0 + 1:c0 + 2], None, ALU.is_ge),
                reads=[R(e_sb, 0, 128, ei0, ei1), R(rsc2, 0, 128, c0 + 1, c0 + 2)],
                writes=[R(rkeep, 0, 128, 0, E)])
            sc.op("v", lambda v, ei0=ei0, ei1=ei1, c0=c0: v.scalar_tensor_tensor(
                comb[:, ei0:ei1], e_sb[:, ei0:ei1], rsc2[:, c0 + 3:c0 + 4], rkeep[:, :],
                ALU.mult, ALU.mult),
                reads=[R(e_sb, 0, 128, ei0, ei1), R(rsc2, 0, 128, c0 + 3, c0 + 4),
                       R(rkeep, 0, 128, 0, E)],
                writes=[R(comb, 0, 128, ei0, ei1)])
            sc.op("v", lambda v, ei0=ei0, ei1=ei1: v.tensor_copy(comb_bf[:, ei0:ei1], comb[:, ei0:ei1]),
                  reads=[R(comb, 0, 128, ei0, ei1)], writes=[R(comb_bf, 0, 128, ei0, ei1)])

        def router_fin(i):
            ei0, ei1 = i * E, (i + 1) * E
            pt = ptrs[i % 2]
            sc.op("t", lambda t, ei0=ei0, ei1=ei1, i=i, pt=pt: t.transpose(
                pt[0:8, 0:128], comb_bf[:, ei0:ei1], identb_sb[:, :]),
                reads=[R(comb_bf, 0, 128, ei0, ei1), R(identb_sb, 0, 128, 0, 128)],
                writes=[R(pt, 0, 128, 0, 1024)])
            sc.op("s", lambda s, i=i, pt=pt: s.activation(
                combT[0:8, i * 128:(i + 1) * 128], pt[0:8, 0:128],
                AF.Identity, bias=0.0, scale=1.0),
                reads=[R(pt, 0, 128, 0, 1024)],
                writes=[R(combT, 0, 8, i * 128, (i + 1) * 128)])

        def b2_init():
            for idx, (i, n) in enumerate([(i, n) for i in range(NSUB) for n in range(2)]):
                bank = py[(2 * i + n) % 4]
                sc.op("t", lambda t, i=i, n=n, bank=bank: t.matmul(
                    bank[:, 0:512], combT[0:8, i * 128:(i + 1) * 128],
                    b2_bf[0:8, n * 512:(n + 1) * 512], start=True, stop=True),
                    reads=[R(combT, 0, 8, i * 128, (i + 1) * 128), R(b2_bf, 0, 8, n * 512, (n + 1) * 512)],
                    writes=[R(bank, 0, 128, 0, 512)])
                sc.op("s", lambda s, i=i, n=n, bank=bank: s.activation(
                    acc_sb[:, i * D + n * 512: i * D + (n + 1) * 512], bank[:, 0:512],
                    AF.Identity, bias=0.0, scale=1.0),
                    reads=[R(bank, 0, 128, 0, 512)],
                    writes=[R(acc_sb, 0, 128, i * D + n * 512, i * D + (n + 1) * 512)])

        w1_chunk_idx = [0]
        w2_chunk_idx = [0]

        def load_w1_chunk(t, e, q):
            par = w1_chunk_idx[0] % 2
            w1_chunk_idx[0] += 1
            dst = w1b[par]
            wreg = [R(dst, 0, 128, 0, KD * 1024)]
            bsem = ("w1a", "w1b")[par]
            if t == 0:
                src = dr3(w1_d[e, :, q * 1024:(q + 1) * 1024])
                gdma(sb3(dst[:, :], KD), src, [DR(w1_d, e * 4 + q, e * 4 + q + 1)], wreg,
                     sem=("c1a", "c1b")[par])
                sdst = dr3(w1bf_d[e, :, q * 1024:(q + 1) * 1024])
                ydma(sdst, sb3(dst[:, :], KD), wreg, [DR(w1bf_d, e * 4 + q, e * 4 + q + 1)],
                     sem=("s1a", "s1b")[par])
            else:
                src = dr3(w1bf_d[e, :, q * 1024:(q + 1) * 1024])
                ydma(sb3(dst[:, :], KD), src, [DR(w1bf_d, e * 4 + q, e * 4 + q + 1)], wreg, sem=bsem)
            return par

        def load_w2_chunk(t, e, n, q):
            par = w2_chunk_idx[0] % 2
            w2_chunk_idx[0] += 1
            dst = w2b[par]
            wreg = [R(dst, 0, 128, 0, 8 * 512)]
            ci = e * 8 + n * 4 + q
            bsem = ("w2a", "w2b")[par]
            if t == 0:
                src = dr3(w2_d[e, q * 1024:(q + 1) * 1024, n * 512:(n + 1) * 512])
                gdma(sb3(dst[:, :], 8), src, [DR(w2_d, ci, ci + 1)], wreg,
                     sem=("c2a", "c2b")[par])
                sdst = dr3(w2bf_d[e, q * 1024:(q + 1) * 1024, n * 512:(n + 1) * 512])
                ydma(sdst, sb3(dst[:, :], 8), wreg, [DR(w2bf_d, ci, ci + 1)],
                     sem=("s2a", "s2b")[par])
            else:
                src = dr3(w2bf_d[e, q * 1024:(q + 1) * 1024, n * 512:(n + 1) * 512])
                ydma(sb3(dst[:, :], 8), src, [DR(w2bf_d, ci, ci + 1)], wreg, sem=bsem)
            return par

        p1banks = [pb[0], pb[1], py[2], py[3]]

        def expert_p1(t, e):
            def p1_mm(f, par):
                bank = p1banks[f % len(p1banks)]

                def mm1(tt, par=par, fl=f % 8, bank=bank):
                    ins = None
                    for k in range(KD):
                        ins = tt.matmul(bank[:, 0:512],
                                        w1b[par][:, k * 1024 + fl * 128: k * 1024 + (fl + 1) * 128],
                                        lnT[:, k * TT:(k + 1) * TT],
                                        start=(k == 0), stop=(k == KD - 1))
                    return ins
                sc.op("t", mm1,
                      reads=[R(w1b[par], 0, 128, 0, KD * 1024), R(lnT, 0, 128, 0, KD * TT)],
                      writes=[R(bank, 0, 128, 0, 512)])

            def p1_sX(f):
                bank = p1banks[f % len(p1banks)]
                p = f % 2
                xs = slice(p * 512, (p + 1) * 512)
                xreg = R(xg, 0, 128, p * 512, (p + 1) * 512)
                bcol = f * E + e
                sc.op("s", lambda s, xs=xs, bank=bank, bcol=bcol: s.activation(
                    xg[:, xs], bank[:, 0:512], AF.Identity, bias=b1col[:, bcol:bcol + 1], scale=1.0),
                    reads=[R(bank, 0, 128, 0, 512), R(b1col, 0, 128, bcol, bcol + 1)],
                    writes=[xreg])

            def p1_vUV(f):
                p = f % 2
                xs = slice(p * 512, (p + 1) * 512)
                xreg = R(xg, 0, 128, p * 512, (p + 1) * 512)
                ureg = R(ug, 0, 128, p * 512, (p + 1) * 512)
                vreg = R(vg, 0, 128, p * 512, (p + 1) * 512)
                sc.op("v", lambda v, xs=xs: v.tensor_tensor(
                    ug[:, xs], xg[:, xs], xg[:, xs], ALU.mult),
                    reads=[xreg], writes=[ureg])
                sc.op("v", lambda v, xs=xs: v.tensor_scalar(
                    ug[:, xs], ug[:, xs], GC1, GC2, ALU.mult, ALU.add),
                    reads=[ureg], writes=[ureg])
                sc.op("v", lambda v, xs=xs: v.tensor_tensor(
                    vg[:, xs], ug[:, xs], xg[:, xs], ALU.mult),
                    reads=[ureg, xreg], writes=[vreg])

            def p1_sC(f):
                p = f % 2
                xs = slice(p * 512, (p + 1) * 512)
                vreg = R(vg, 0, 128, p * 512, (p + 1) * 512)
                sc.op("s", lambda s, xs=xs: s.activation(vg[:, xs], vg[:, xs], AF.Sigmoid),
                      reads=[vreg], writes=[vreg])

            def p1_gD(f):
                p = f % 2
                xs = slice(p * 512, (p + 1) * 512)
                sc.op("v", lambda v, xs=xs, f=f: v.tensor_tensor(
                    hT[:, f * TT:(f + 1) * TT], vg[:, xs], xg[:, xs], ALU.mult),
                    reads=[R(vg, 0, 128, p * 512, (p + 1) * 512),
                           R(xg, 0, 128, p * 512, (p + 1) * 512)],
                    writes=[R(hT, 0, 128, f * TT, (f + 1) * TT)])

            for q in range(4):
                par = load_w1_chunk(t, e, q)
                for fl in range(8):
                    f = q * 8 + fl
                    p1_mm(f, par)
                    p1_sX(f)
                    if f >= 1:
                        p1_sC(f - 1)
                    p1_vUV(f)
                    if f >= 1:
                        p1_gD(f - 1)
            p1_sC(NF - 1)
            p1_gD(NF - 1)

        def expert_p2(t, e):
            for n in range(2):
                for q in range(4):
                    par = load_w2_chunk(t, e, n, q)
                    for i in range(NSUB):

                        def mm2(tt, par=par, q=q, i=i, n=n):
                            ins = None
                            for j in range(8):
                                f = q * 8 + j
                                ins = tt.matmul(
                                    py[i][:, 0:512],
                                    hT[:, f * TT + i * 128: f * TT + (i + 1) * 128],
                                    w2b[par][:, j * 512:(j + 1) * 512],
                                    start=(q == 0 and j == 0), stop=(q == 3 and j == 7),
                                    skip_group_check=True)
                            return ins
                        sc.op("t", mm2,
                              reads=[R(hT, 0, 128, q * 8 * TT, (q + 1) * 8 * TT),
                                     R(w2b[par], 0, 128, 0, 8 * 512)],
                              writes=[R(py[i], 0, 128, 0, 512)])
                        if q == 3:
                            ccol = i * E + e
                            d0 = i * D + n * 512
                            sc.op("v", lambda v, i=i, ccol=ccol, d0=d0: v.scalar_tensor_tensor(
                                acc_sb[:, d0:d0 + 512], py[i][:, 0:512], comb[:, ccol:ccol + 1],
                                acc_sb[:, d0:d0 + 512], ALU.mult, ALU.add),
                                reads=[R(py[i], 0, 128, 0, 512), R(comb, 0, 128, ccol, ccol + 1),
                                       R(acc_sb, 0, 128, d0, d0 + 512)],
                                writes=[R(acc_sb, 0, 128, d0, d0 + 512)])

        def ffn_sub(t, i, xbuf):
            bnk = pb[i % 2]

            def rmm(tt, i=i, bnk=bnk):
                ins = None
                for k in range(KD):
                    ins = tt.matmul(bnk[0:16, 0:128],
                                    a_bf[:, k * 16:(k + 1) * 16],
                                    lnT[:, k * TT + i * 128: k * TT + (i + 1) * 128],
                                    start=(k == 0), stop=(k == KD - 1))
                return ins
            sc.op("t", rmm,
                  reads=[R(a_bf, 0, 128, 0, KD * 16)]
                        + [R(lnT, 0, 128, k * TT + i * 128, k * TT + (i + 1) * 128) for k in range(KD)],
                  writes=[R(bnk, 0, 128, 0, 512)])
            sc.op("s", lambda s, i=i, bnk=bnk: s.activation(
                rT_sb[0:16, i * 128:(i + 1) * 128], bnk[0:16, 0:128],
                AF.Identity, bias=0.0, scale=1.0),
                reads=[R(bnk, 0, 128, 0, 512)],
                writes=[R(rT_sb, 0, 16, i * 128, (i + 1) * 128)])
            for n in range(2):
                bank = py[(2 * i + n) % 4]

                def fmm(tt, i=i, n=n, bank=bank):
                    tt.matmul(bank[:, 0:512], onesb[0:1, 0:128],
                              fbrow[0:1, n * 512:(n + 1) * 512], start=True, stop=False)
                    for k in range(KD):
                        tt.matmul(bank[:, 0:512],
                                  lnT[:, k * TT + i * 128: k * TT + (i + 1) * 128],
                                  wqb[:, k * 1024 + n * 512: k * 1024 + (n + 1) * 512],
                                  start=False, stop=False)
                    return tt.matmul(bank[:, 0:512], rT_sb[0:16, i * 128:(i + 1) * 128],
                                     bb_bf[0:16, n * 512:(n + 1) * 512], start=False, stop=True)
                sc.op("t", fmm,
                      reads=[R(onesb, 0, 1, 0, 128), R(fbrow, 0, 1, n * 512, (n + 1) * 512),
                             R(wqb, 0, 128, 0, KD * 1024),
                             R(rT_sb, 0, 16, i * 128, (i + 1) * 128),
                             R(bb_bf, 0, 16, n * 512, (n + 1) * 512)]
                            + [R(lnT, 0, 128, k * TT + i * 128, k * TT + (i + 1) * 128) for k in range(KD)],
                      writes=[R(bank, 0, 128, 0, 512)])
                d0 = i * D + n * 512
                sc.op("v", lambda v, bank=bank, d0=d0, xbuf=xbuf: v.tensor_tensor(
                    out_sb[:, d0:d0 + 512], bank[:, 0:512], xbuf[:, d0:d0 + 512], ALU.add),
                    reads=[R(bank, 0, 128, 0, 512), R(xbuf, 0, 128, d0, d0 + 512)],
                    writes=[R(out_sb, 0, 128, d0, d0 + 512)])

        def load_x(t):
            xbuf = xb[t % 2]
            gdma(sb3(xbuf[:, :], NSUB), dr3(x_d[t * TT:(t + 1) * TT, :]),
                 [DR(x_d, t, t + 1)], [R(xbuf, 0, 128, 0, NSUB * D)],
                 sem=("gxa", "gxb")[t % 2])

        load_x(0)
        for i in range(NSUB):
            layernorm(i, xb[0])
        preamble_b1()
        preamble_quant()
        for t in range(nt):
            x_sb = xb[t % 2]
            if do_router:
                for i in range(NSUB):
                    router(i)
            else:
                for idx in range(8):
                    i, n = idx // 2, idx % 2
                    sc.op("v", lambda v, i=i, n=n: v.memset(
                        acc_sb[:, i * D + n * 512: i * D + (n + 1) * 512], 0.0),
                        reads=[], writes=[R(acc_sb, 0, 128, i * D + n * 512, i * D + (n + 1) * 512)])
                sc.op("v", lambda v: v.memset(comb[:, :], 0.25),
                      reads=[], writes=[R(comb, 0, 128, 0, NSUB * E)])
                sc.op("v", lambda v: v.tensor_copy(comb_bf[:, :], comb[:, :]),
                      reads=[R(comb, 0, 128, 0, NSUB * E)], writes=[R(comb_bf, 0, 128, 0, NSUB * E)])
            for e in range(ne):
                expert_p1(t, e)
                if e == 0:
                    for i in range(NSUB):
                        router_fin(i)
                    b2_init()
                expert_p2(t, e)
            for i in range(NSUB):
                sc.op("v", lambda v, i=i, x_sb=x_sb: v.tensor_tensor(
                    x_sb[:, i * D:(i + 1) * D], x_sb[:, i * D:(i + 1) * D],
                    acc_sb[:, i * D:(i + 1) * D], ALU.add),
                    reads=[R(x_sb, 0, 128, i * D, (i + 1) * D), R(acc_sb, 0, 128, i * D, (i + 1) * D)],
                    writes=[R(x_sb, 0, 128, i * D, (i + 1) * D)])
                if do_ffn:
                    layernorm(i, x_sb)
                    ffn_sub(t, i, x_sb)
            if t + 1 < nt:
                load_x(t + 1)
                for i in range(NSUB):
                    layernorm(i, xb[(t + 1) % 2])
            if do_debug and t == 0:
                gdma(dbg["dbg_x1"], x_sb[:, :], [R(x_sb, 0, 128, 0, 4096)], [], sem="dbg")
            if not do_ffn:
                for idx in range(8):
                    i, n = idx // 2, idx % 2
                    d0 = i * D + n * 512
                    sc.op("v", lambda v, d0=d0: v.tensor_copy(
                        out_sb[:, d0:d0 + 512], x_sb[:, d0:d0 + 512]),
                        reads=[R(x_sb, 0, 128, d0, d0 + 512)],
                        writes=[R(out_sb, 0, 128, d0, d0 + 512)])
            gdma(dr3(y_d[t * TT:(t + 1) * TT, :]), sb3(out_sb[:, :], NSUB),
                 [R(out_sb, 0, 128, 0, NSUB * D)], [DR(y_d, t, t + 1)], sem="gy")

        if do_debug:
            if do_router:
                gdma(dbg["dbg_comb"], comb[:, :], [R(comb, 0, 128, 0, 32)], [], sem="dbg")
                gdma(dbg["dbg_e"], e_sb[:, :], [R(e_sb, 0, 128, 0, 32)], [], sem="dbg")
                sc.op("v", lambda v: v.tensor_copy(xg[0:8, 0:512], combT[0:8, :]),
                      reads=[R(combT, 0, 8, 0, 512)], writes=[R(xg, 0, 8, 0, 512)])
                gdma(dbg["dbg_combT"], xg[0:8, 0:512], [R(xg, 0, 8, 0, 512)], [], sem="dbg")
                gdma(dbg["dbg_rsc"], rsc[:, :], [R(rsc, 0, 128, 0, 16)], [], sem="dbg")
            gdma(dbg["dbg_acc"], acc_sb[:, :], [R(acc_sb, 0, 128, 0, 4096)], [], sem="dbg")
            sc.op("v", lambda v: v.tensor_copy(xg[:, 0:512], rT_sb[0:16, :].rearrange("p (a c) -> p a c", a=1)[:, 0, :]) if False else v.tensor_copy(ug[0:16, 0:512], rT_sb[0:16, :]),
                  reads=[R(rT_sb, 0, 16, 0, TT)], writes=[R(ug, 0, 16, 0, 512)])
            gdma(dbg["dbg_rT"], ug[0:16, 0:512], [R(ug, 0, 16, 0, 512)], [], sem="dbg")
            sc.op("v", lambda v: v.tensor_copy(vg[:, 0:512], wqb[:, 0:512]),
                  reads=[R(wqb, 0, 128, 0, 512)], writes=[R(vg, 0, 128, 0, 512)])
            gdma(dbg["dbg_wq"][:, 0:512], vg[:, 0:512], [R(vg, 0, 128, 0, 512)], [], sem="dbg")
            gdma(dbg["dbg_qsc"], qsc[:, :], [R(qsc, 0, 128, 0, 2)], [], sem="dbg")
            gdma(dbg["dbg_lnT"], lnT[:, :].rearrange("p (a c) -> p a c", a=KD),
                 [R(lnT, 0, 128, 0, KD * TT)], [], sem="dbg")
            if ne > 0:
                gdma(dbg["dbg_hT"], hT[:, 0:2048].rearrange("p (a c) -> p a c", a=4),
                     [R(hT, 0, 128, 0, 2048)], [], sem="dbg")

        for smm in Sched.DMA_SEMS:
            if sc.cnt[smm] > 0:
                sc.raw_wait("g", smm, sc.cnt[smm])

        with nc.Block() as blk:
            @blk.tensor
            def _(engine):
                sc.replay("t", engine, sems)

            @blk.vector
            def _(engine):
                sc.replay("v", engine, sems)

            @blk.scalar
            def _(engine):
                sc.replay("s", engine, sems)

            @blk.gpsimd
            def _(engine):
                sc.replay("g", engine, sems)

            @blk.sync
            def _(engine):
                sc.replay("y", engine, sems)

    return nc


_NC_CACHE = {}


def kernel(L0_attn_hidden_states, L0_attn_gate_w, L0_attn_w1, L0_attn_b1,
           L0_attn_w2, L0_attn_b2, L0_ffn_kernel, L0_ffn_bias, L0_ffn_a, L0_ffn_b):
    x = np.ascontiguousarray(np.asarray(L0_attn_hidden_states, dtype=np.float32))
    xf = x.reshape(B * S, D)
    gate = np.ascontiguousarray(np.asarray(L0_attn_gate_w, dtype=np.float32))
    w1 = np.ascontiguousarray(np.asarray(L0_attn_w1, dtype=np.float32))
    b1 = np.ascontiguousarray(np.asarray(L0_attn_b1, dtype=np.float32))
    w2 = np.ascontiguousarray(np.asarray(L0_attn_w2, dtype=np.float32))
    b2 = np.ascontiguousarray(np.asarray(L0_attn_b2, dtype=np.float32))
    fk = np.ascontiguousarray(np.asarray(L0_ffn_kernel, dtype=np.float32))
    fb = np.ascontiguousarray(np.asarray(L0_ffn_bias, dtype=np.float32)).reshape(1, D)
    fa = np.ascontiguousarray(np.asarray(L0_ffn_a, dtype=np.float32))
    fbb = np.ascontiguousarray(np.asarray(L0_ffn_b, dtype=np.float32))
    identb = np.eye(128, dtype=ml_dtypes.bfloat16)
    identf = np.eye(128, dtype=np.float32)
    consts = np.tile(np.array([[1e-5, -0.5, GC1, GC2]], dtype=np.float32), (128, 1))
    onesf = np.ones((1, 512), dtype=np.float32)

    if "nc" not in _NC_CACHE:
        _NC_CACHE["nc"] = build_program()
    nc = _NC_CACHE["nc"]

    in_maps = []
    for c in range(NCORES):
        in_maps.append({
            "x": np.ascontiguousarray(xf[c * TC:(c + 1) * TC]),
            "gate": gate, "w1": w1, "b1": b1, "w2": w2, "b2": b2,
            "fk": fk, "fb": fb, "fa": fa, "fbb": fbb,
            "identb": identb, "identf": identf, "consts": consts, "onesf": onesf,
        })
    res = run_bass_kernel_spmd(nc, in_maps, core_ids=list(range(NCORES)))
    out = np.concatenate([res.results[c]["y"] for c in range(NCORES)], axis=0)
    return out.reshape(B, S, D)


# version 22 (final)
# speedup vs baseline: 1.1929x; 1.0142x over previous
import sys

for _p in ("/opt/trn_rl_repo",):
    if _p not in sys.path:
        sys.path.insert(0, _p)

import numpy as np
import ml_dtypes
from contextlib import ExitStack

import concourse.bass as bass
import concourse.mybir as mybir
import concourse.bass_utils as _bass_utils
from concourse.bass_utils import run_bass_kernel_spmd

try:
    _orig_bvo = _bass_utils.bir_verify_and_optimise
    _orig_run_command = _bass_utils.run_command

    def _bvo_ldw(*args, **kwargs):
        def patched_run(cmd, **kw):
            try:
                cmd = [c.replace("--enable-ldw-opt=false", "--enable-ldw-opt=true")
                       if isinstance(c, str) else c for c in cmd]
            except Exception:
                pass
            return _orig_run_command(cmd, **kw)

        _bass_utils.run_command = patched_run
        try:
            return _orig_bvo(*args, **kwargs)
        finally:
            _bass_utils.run_command = _orig_run_command

    _bass_utils.bir_verify_and_optimise = _bvo_ldw
except Exception:
    pass

F32 = mybir.dt.float32
BF16 = mybir.dt.bfloat16
I32 = mybir.dt.int32
AF = mybir.ActivationFunctionType
ALU = mybir.AluOpType
AX = mybir.AxisListType.X

NCORES = 8
B, S, D, E, FF = 8, 4096, 1024, 8, 4096
TC = B * S // NCORES
TT = 512
NT = TC // TT
NSUB = TT // 128
KD = D // 128
NF = FF // 128
GC1 = 0.071354816
GC2 = 1.5957691216


class Sched:

    ENGINES = ("t", "v", "s", "g", "y")
    SELF_EDGES = {"t": False, "v": True, "s": True, "g": True, "y": True}
    SELF_SKIP_MARGIN = {"v": 16, "s": 16}
    INC = {"t": 1, "v": 1, "s": 1, "g": 16, "y": 16}
    DMA_SEMS = ("gc", "gx", "gy", "kq", "b1a", "b1b", "w1a", "w1b", "w2a", "w2b",
                "s1a", "s1b", "s2a", "s2b", "c1a", "c1b", "c2a", "c2b",
                "gxa", "gxb", "dbg")

    def __init__(self):
        self.prog = {e: [] for e in self.ENGINES}
        self.cnt = {e: 0 for e in self.ENGINES}
        for smm in self.DMA_SEMS:
            self.cnt[smm] = 0
        self.vc = {e: {} for e in self.ENGINES}
        self.snap = {e: {} for e in self.ENGINES}
        for smm in self.DMA_SEMS:
            self.snap[smm] = {}
        self.writers = {}
        self.readers = {}

    @staticmethod
    def _overlap(a, b):
        return a[0] < b[1] and b[0] < a[1] and a[2] < b[3] and b[2] < a[3]

    @staticmethod
    def _covers(a, b):
        return a[0] <= b[0] and a[1] >= b[1] and a[2] <= b[2] and a[3] >= b[3]

    def _need_from(self, regions, want_writers, want_readers, need):
        for r in regions:
            key, ival = r[0], r[1:]
            if want_writers:
                for w in self.writers.get(key, ()):
                    if self._overlap(ival, w[:4]):
                        e, c = w[4], w[5]
                        if need.get(e, 0) < c:
                            need[e] = c
            if want_readers:
                for (rival, e), c in self.readers.get(key, {}).items():
                    if self._overlap(ival, rival):
                        if need.get(e, 0) < c:
                            need[e] = c

    def _snap_at(self, sem, cnt):
        sn = self.snap[sem]
        best = None
        bk = -1
        for k, v in sn.items():
            if bk < k <= cnt:
                bk, best = k, v
        return best

    def op(self, eng, emit, reads=(), writes=(), inc=None, sem=None, defer_commit=False):
        mysem = sem if sem is not None else eng
        need = {}
        self._need_from(reads, True, False, need)
        self._need_from(writes, True, True, need)
        vc = self.vc[eng]
        margin = self.SELF_SKIP_MARGIN.get(eng)
        waits = []
        for dsem, dcnt in sorted(need.items()):
            if dsem == mysem and mysem == eng:
                if not self.SELF_EDGES[eng]:
                    continue
                if margin is not None and self.cnt[eng] - dcnt >= margin:
                    continue
            if vc.get(dsem, 0) >= dcnt:
                continue
            waits.append((dsem, dcnt))
        for dsem, dcnt in waits:
            if vc.get(dsem, 0) < dcnt:
                vc[dsem] = dcnt
            sn = self._snap_at(dsem, dcnt)
            if sn:
                for k, v in sn.items():
                    if vc.get(k, 0) < v:
                        vc[k] = v
        step = self.INC[eng] if inc is None else inc
        self.cnt[mysem] += step
        newcnt = self.cnt[mysem]
        self.prog[eng].append((tuple(waits), emit, mysem, step))
        sn = dict(vc)
        sn[mysem] = newcnt
        self.snap[mysem][newcnt] = sn
        if not defer_commit:
            self.commit(mysem, newcnt, reads, writes)
        return newcnt

    def commit(self, mysem, newcnt, reads, writes):
        for r in reads:
            key, ival = r[0], r[1:]
            rd = self.readers.setdefault(key, {})
            k2 = (ival, mysem)
            if rd.get(k2, 0) < newcnt:
                rd[k2] = newcnt
        for r in writes:
            key, ival = r[0], r[1:]
            ws = self.writers.setdefault(key, [])
            ws[:] = [w for w in ws if not self._covers(ival, w[:4])]
            ws.append((*ival, mysem, newcnt))
            rd = self.readers.get(key)
            if rd:
                for k2 in [k for k in rd if self._covers(ival, k[0])]:
                    del rd[k2]

    def raw_wait(self, eng, dsem, dcnt):
        self.prog[eng].append((((dsem, dcnt),), None, eng, 0))

    def replay(self, eng, engine, sems):
        for waits, emit, mysem, step in self.prog[eng]:
            for dsem, dcnt in waits:
                engine.wait_ge(sems[dsem], dcnt)
            if emit is not None:
                ins = emit(engine)
                if step:
                    ins.then_inc(sems[mysem], step)


def R(t, p0, p1, f0, f1):
    return (id(t), p0, p1, f0, f1)


def dr3(dram_slice, p=128):
    return dram_slice.rearrange("(a p) c -> p a c", p=p)


def sb3(sbuf_ap, a):
    return sbuf_ap.rearrange("p (a c) -> p a c", a=a)


def build_program(nt=NT, ne=E, do_router=True, do_ffn=True, do_quant=True, do_b1prep=True, do_smalls=True, do_debug=False):
    nc = bass.Bass("TRN2", target_bir_lowering=False)

    x_d = nc.dram_tensor("x", [TC, D], F32, kind="ExternalInput").ap()
    gate_d = nc.dram_tensor("gate", [D, E], F32, kind="ExternalInput").ap()
    w1_d = nc.dram_tensor("w1", [E, D, FF], F32, kind="ExternalInput").ap()
    b1_d = nc.dram_tensor("b1", [E, FF], F32, kind="ExternalInput").ap()
    w2_d = nc.dram_tensor("w2", [E, FF, D], F32, kind="ExternalInput").ap()
    b2_d = nc.dram_tensor("b2", [E, D], F32, kind="ExternalInput").ap()
    fk_d = nc.dram_tensor("fk", [D, D], F32, kind="ExternalInput").ap()
    fb_d = nc.dram_tensor("fb", [1, D], F32, kind="ExternalInput").ap()
    fa_d = nc.dram_tensor("fa", [D, 16], F32, kind="ExternalInput").ap()
    fbb_d = nc.dram_tensor("fbb", [16, D], F32, kind="ExternalInput").ap()
    identb_d = nc.dram_tensor("identb", [128, 128], BF16, kind="ExternalInput").ap()
    identf_d = nc.dram_tensor("identf", [128, 128], F32, kind="ExternalInput").ap()
    consts_d = nc.dram_tensor("consts", [128, 4], F32, kind="ExternalInput").ap()
    onesf_d = nc.dram_tensor("onesf", [1, 512], F32, kind="ExternalInput").ap()
    y_d = nc.dram_tensor("y", [TC, D], F32, kind="ExternalOutput").ap()
    dbg = {}
    if do_debug:
        for nm, shape, dt in [("dbg_comb", [128, 32], F32), ("dbg_e", [128, 32], F32),
                              ("dbg_combT", [8, 512], F32), ("dbg_acc", [128, 4096], F32),
                              ("dbg_x1", [128, 4096], F32), ("dbg_lnT", [128, 4096], F32),
                              ("dbg_hT", [128, 2048], F32), ("dbg_rsc", [128, 16], F32),
                              ("dbg_wq", [128, 8192], F32), ("dbg_qsc", [128, 2], F32),
                              ("dbg_rT", [16, 512], F32)]:
            dbg[nm] = nc.dram_tensor(nm, shape, dt, kind="ExternalOutput").ap()
    w1bf_d = nc.dram_tensor("w1bf", [E, D, FF], BF16, kind="Internal").ap()
    w2bf_d = nc.dram_tensor("w2bf", [E, FF, D], BF16, kind="Internal").ap()

    es = ExitStack()
    with es:
        sb = lambda name, shape, dt: es.enter_context(nc.sbuf_tensor(name, shape, dt))
        ps = lambda name, shape, dt: es.enter_context(nc.psum_tensor(name, shape, dt))

        xb = [sb("x_sbA", [128, NSUB * D], F32), sb("x_sbB", [128, NSUB * D], F32)]
        out_sb = sb("out_sb", [128, NSUB * D], F32)
        acc_sb = sb("acc_sb", [128, NSUB * D], F32)
        lnst = sb("lnst", [128, NSUB * D], BF16)
        lnT = sb("lnT", [128, KD * TT], BF16)
        hT = sb("hT", [128, NF * TT], BF16)
        w1b = [sb(f"w1b{i}", [128, KD * 1024], BF16) for i in range(2)]
        w2b = [sb(f"w2b{i}", [128, 8 * 512], BF16) for i in range(2)]
        wqb = sb("wqb", [128, KD * 1024], BF16)
        kstage = sb("kstage", [128, 2048], F32)
        xg = sb("xg", [128, 2 * 512], BF16)
        ug = sb("ug", [128, 2 * 512], BF16)
        vg = sb("vg", [128, 2 * 512], BF16)
        b1st = sb("b1st", [8, 2 * 128], F32)
        b1col = sb("b1col", [128, NF * E], F32)
        gate_bf = sb("gate_bf", [128, KD * E], BF16)
        b2_bf = sb("b2_bf", [8, D], BF16)
        a_bf = sb("a_bf", [128, KD * 16], BF16)
        bb_bf = sb("bb_bf", [16, D], BF16)
        fbrow = sb("fbrow", [1, D], BF16)
        onesb = sb("onesb", [1, 512], BF16)
        identb_sb = sb("identb_sb", [128, 128], BF16)
        identf_sb = sb("identf_sb", [128, 128], F32)
        consts_sb = sb("consts_sb", [128, 4], F32)
        stats = sb("stats", [128, 12], F32)
        mv = sb("mv", [128, 2], F32)
        lnsc = sb("lnsc", [128, 4], F32)
        e_sb = sb("e_sb", [128, NSUB * E], F32)
        rsc = sb("rsc", [128, NSUB * 4], F32)
        rsc2 = sb("rsc2", [128, NSUB * 4], F32)
        ris = sb("ris", [128, E], F32)
        rwo = sb("rwo", [128, E], F32)
        rkeep = sb("rkeep", [128, E], F32)
        comb = sb("comb", [128, NSUB * E], F32)
        comb_bf = sb("comb_bf", [128, NSUB * E], BF16)
        combT = sb("combT", [8, NSUB * 128], BF16)
        rT_sb = sb("rT_sb", [16, TT], BF16)
        qsc = sb("qsc", [128, 2], F32)
        qrow = sb("qrow", [1, 4], F32)
        qhi = sb("qhi", [1, 2], BF16)
        qhf = sb("qhf", [1, 2], F32)
        qdf = sb("qdf", [1, 2], F32)
        qlo = sb("qlo", [1, 2], BF16)
        cmax = sb("cmax", [128, 4], F32)
        qmax = sb("qmax", [128, 1], F32)

        pb = [ps(f"pb{i}", [128, 512], F32) for i in range(2)]
        py = [ps(f"py{i}", [128, 512], F32) for i in range(4)]
        ptrs = [ps("ptra", [128, 1024], BF16), ps("ptrb", [128, 1024], BF16)]

        sems = {}
        for nm in Sched.ENGINES + Sched.DMA_SEMS:
            sems[nm] = es.enter_context(nc.semaphore(f"sem_{nm}"))

        sc = Sched()

        def gdma(dst_ap, src_ap, reads, writes, sem="gx", defer_commit=False):
            return sc.op("g", lambda g, d=dst_ap, s=src_ap: g.dma_start(d, s),
                         reads=reads, writes=writes, sem=sem, defer_commit=defer_commit)

        def ydma(dst_ap, src_ap, reads, writes, sem="kq"):
            return sc.op("y", lambda y, d=dst_ap, s=src_ap: y.dma_start(d, s),
                         reads=reads, writes=writes, sem=sem)

        DR = lambda t, i0=0, i1=1: (("dram", id(t)), 0, 1, i0, i1)

        const_writes = []

        def cdma(dst_ap, src_ap, rd, wr):
            gdma(dst_ap, src_ap, rd, wr, sem="gc", defer_commit=True)
            const_writes.extend(wr)

        cdma(identb_sb[:, :], identb_d, [DR(identb_d)], [R(identb_sb, 0, 128, 0, 128)])
        cdma(identf_sb[:, :], identf_d, [DR(identf_d)], [R(identf_sb, 0, 128, 0, 128)])
        cdma(consts_sb[:, :], consts_d, [DR(consts_d)], [R(consts_sb, 0, 128, 0, 4)])
        cdma(onesb[:, :], onesf_d, [DR(onesf_d)], [R(onesb, 0, 1, 0, 512)])
        if do_smalls:
            cdma(sb3(gate_bf[:, :], KD), dr3(gate_d), [DR(gate_d)], [R(gate_bf, 0, 128, 0, KD * E)])
            cdma(b2_bf[:, :], b2_d, [DR(b2_d)], [R(b2_bf, 0, 8, 0, D)])
            cdma(sb3(a_bf[:, :], KD), dr3(fa_d), [DR(fa_d)], [R(a_bf, 0, 128, 0, KD * 16)])
            cdma(bb_bf[:, :], fbb_d, [DR(fbb_d)], [R(bb_bf, 0, 16, 0, D)])
            cdma(fbrow[:, :], fb_d, [DR(fb_d)], [R(fbrow, 0, 1, 0, D)])
        sc.commit("gc", sc.cnt["gc"], [], const_writes)

        def preamble_b1():
            for f in range(NF if do_b1prep else 0):
                par = f % 2
                gdma(b1st[:, par * 128:(par + 1) * 128],
                     b1_d[:, f * 128:(f + 1) * 128],
                     [DR(b1_d)], [R(b1st, 0, 8, par * 128, (par + 1) * 128)],
                     sem=("b1a", "b1b")[par])
                slot = f % 8
                sc.op("t", lambda t, par=par, slot=slot: t.transpose(
                    pb[0][:, slot * 8:(slot + 1) * 8],
                    b1st[0:8, par * 128:(par + 1) * 128],
                    identf_sb[0:8, 0:8]),
                    reads=[R(b1st, 0, 8, par * 128, (par + 1) * 128),
                           R(identf_sb, 0, 8, 0, 8)],
                    writes=[R(pb[0], 0, 128, slot * 8, (slot + 1) * 8)])
                if slot == 7:
                    grp = f // 8
                    sc.op("s", lambda s, grp=grp: s.activation(
                        b1col[:, grp * 64:(grp + 1) * 64], pb[0][:, 0:64],
                        AF.Identity, bias=0.0, scale=1.0),
                        reads=[R(pb[0], 0, 128, 0, 64)],
                        writes=[R(b1col, 0, 128, grp * 64, (grp + 1) * 64)])

        def preamble_quant():
            def load_fk_quarter(h):
                ydma(sb3(kstage[:, :], 2), dr3(fk_d[h * 256:(h + 1) * 256, :]),
                     [DR(fk_d)], [R(kstage, 0, 128, 0, 2048)])

            for h in range(4 if do_quant else 0):
                load_fk_quarter(h)
                sc.op("v", lambda v, h=h: v.reduce_max(
                    cmax[:, h:h + 1], kstage[:, :], axis=AX, apply_absolute_value=True),
                    reads=[R(kstage, 0, 128, 0, 2048)],
                    writes=[R(cmax, 0, 128, h, h + 1)])
            if not do_quant:
                sc.op("v", lambda v: v.memset(wqb[:, :], 0.0), reads=[], writes=[R(wqb, 0, 128, 0, KD * 1024)])
            if do_quant:
              sc.op("v", lambda v: v.tensor_tensor(qmax[:, 0:1], cmax[:, 0:1], cmax[:, 1:2], ALU.max),
                    reads=[R(cmax, 0, 128, 0, 2)], writes=[R(qmax, 0, 128, 0, 1)])
              sc.op("v", lambda v: v.tensor_tensor(cmax[:, 2:3], cmax[:, 2:3], cmax[:, 3:4], ALU.max),
                    reads=[R(cmax, 0, 128, 2, 4)], writes=[R(cmax, 0, 128, 2, 3)])
              sc.op("v", lambda v: v.tensor_tensor(qmax[:, 0:1], qmax[:, 0:1], cmax[:, 2:3], ALU.max),
                    reads=[R(qmax, 0, 128, 0, 1), R(cmax, 0, 128, 2, 3)],
                    writes=[R(qmax, 0, 128, 0, 1)])
              sc.op("t", lambda t: t.transpose(pb[0][0:1, 0:128], qmax[:, 0:1], identf_sb[:, :]),
                    reads=[R(qmax, 0, 128, 0, 1), R(identf_sb, 0, 128, 0, 128)],
                    writes=[R(pb[0], 0, 1, 0, 128)])
              sc.op("v", lambda v: v.reduce_max(qrow[0:1, 0:1], pb[0][0:1, 0:128], axis=AX),
                    reads=[R(pb[0], 0, 1, 0, 128)], writes=[R(qrow, 0, 1, 0, 1)])
              sc.op("v", lambda v: v.reciprocal(qrow[0:1, 1:2], qrow[0:1, 0:1]),
                    reads=[R(qrow, 0, 1, 0, 1)], writes=[R(qrow, 0, 1, 1, 2)])
              sc.op("v", lambda v: v.tensor_scalar(qrow[0:1, 2:3], qrow[0:1, 1:2], 7.0, None, ALU.mult),
                    reads=[R(qrow, 0, 1, 1, 2)], writes=[R(qrow, 0, 1, 2, 3)])
              sc.op("v", lambda v: v.tensor_scalar(qrow[0:1, 3:4], qrow[0:1, 0:1], 1.0 / 7.0, None, ALU.mult),
                    reads=[R(qrow, 0, 1, 0, 1)], writes=[R(qrow, 0, 1, 3, 4)])
              sc.op("v", lambda v: v.tensor_copy(qhi[0:1, 0:2], qrow[0:1, 2:4]),
                    reads=[R(qrow, 0, 1, 2, 4)], writes=[R(qhi, 0, 1, 0, 2)])
              sc.op("v", lambda v: v.tensor_copy(qhf[0:1, 0:2], qhi[0:1, 0:2]),
                    reads=[R(qhi, 0, 1, 0, 2)], writes=[R(qhf, 0, 1, 0, 2)])
              sc.op("v", lambda v: v.tensor_tensor(qdf[0:1, 0:2], qrow[0:1, 2:4], qhf[0:1, 0:2], ALU.subtract),
                    reads=[R(qrow, 0, 1, 2, 4), R(qhf, 0, 1, 0, 2)], writes=[R(qdf, 0, 1, 0, 2)])
              sc.op("v", lambda v: v.tensor_copy(qlo[0:1, 0:2], qdf[0:1, 0:2]),
                    reads=[R(qdf, 0, 1, 0, 2)], writes=[R(qlo, 0, 1, 0, 2)])

              def qbcast(t):
                  t.matmul(pb[1][:, 0:2], onesb[0:1, 0:128], qhi[0:1, 0:2], start=True, stop=False)
                  return t.matmul(pb[1][:, 0:2], onesb[0:1, 0:128], qlo[0:1, 0:2], start=False, stop=True)

              sc.op("t", qbcast,
                    reads=[R(onesb, 0, 1, 0, 128), R(qhi, 0, 1, 0, 2), R(qlo, 0, 1, 0, 2)],
                    writes=[R(pb[1], 0, 128, 0, 2)])
              sc.op("s", lambda s: s.activation(qsc[:, :], pb[1][:, 0:2], AF.Identity, bias=0.0, scale=1.0),
                    reads=[R(pb[1], 0, 128, 0, 2)], writes=[R(qsc, 0, 128, 0, 2)])

              for h in range(4):
                  load_fk_quarter(h)
                  kr = [R(kstage, 0, 128, 0, 2048)]
                  kw = kr
                  sc.op("v", lambda v: v.tensor_scalar(kstage[:, :], kstage[:, :], qsc[:, 0:1], None, ALU.mult),
                        reads=kr + [R(qsc, 0, 128, 0, 1)], writes=kw)
                  sc.op("v", lambda v: v.tensor_scalar(kstage[:, :], kstage[:, :], 12582912.0, None, ALU.add),
                        reads=kr, writes=kw)
                  sc.op("v", lambda v: v.tensor_scalar(kstage[:, :], kstage[:, :], -12582912.0, None, ALU.add),
                        reads=kr, writes=kw)
                  sc.op("v", lambda v: v.tensor_scalar(kstage[:, :], kstage[:, :], 7.0, None, ALU.min),
                        reads=kr, writes=kw)
                  sc.op("v", lambda v: v.tensor_scalar(kstage[:, :], kstage[:, :], -8.0, None, ALU.max),
                        reads=kr, writes=kw)
                  sc.op("v", lambda v, h=h: v.tensor_scalar(
                      wqb[:, h * 2048:(h + 1) * 2048], kstage[:, :], qsc[:, 1:2], None, ALU.mult),
                      reads=kr + [R(qsc, 0, 128, 1, 2)],
                      writes=[R(wqb, 0, 128, h * 2048, (h + 1) * 2048)])


        def layernorm(i, xbuf):
            xr = R(xbuf, 0, 128, i * D, (i + 1) * D)
            l0 = i * D
            sc.op("v", lambda v, i=i, xbuf=xbuf: v.bn_stats(stats[:, 0:6], xbuf[:, i * D:i * D + 512]),
                  reads=[xr], writes=[R(stats, 0, 128, 0, 6)])
            sc.op("v", lambda v, i=i, xbuf=xbuf: v.bn_stats(stats[:, 6:12], xbuf[:, i * D + 512:(i + 1) * D]),
                  reads=[xr], writes=[R(stats, 0, 128, 6, 12)])
            sc.op("v", lambda v: v.bn_aggr(mv[:, :], stats[:, :]),
                  reads=[R(stats, 0, 128, 0, 12)], writes=[R(mv, 0, 128, 0, 2)])
            sc.op("s", lambda s: s.activation(lnsc[:, 0:1], mv[:, 0:1], AF.Identity, bias=0.0, scale=1.0),
                  reads=[R(mv, 0, 128, 0, 1)], writes=[R(lnsc, 0, 128, 0, 1)])
            sc.op("s", lambda s: s.activation(lnsc[:, 1:2], mv[:, 1:2], AF.Ln, bias=consts_sb[:, 0:1], scale=1.0),
                  reads=[R(mv, 0, 128, 1, 2), R(consts_sb, 0, 128, 0, 1)],
                  writes=[R(lnsc, 0, 128, 1, 2)])
            sc.op("s", lambda s: s.activation(lnsc[:, 2:3], lnsc[:, 1:2], AF.Exp, bias=0.0, scale=consts_sb[:, 1:2]),
                  reads=[R(lnsc, 0, 128, 1, 2), R(consts_sb, 0, 128, 1, 2)],
                  writes=[R(lnsc, 0, 128, 2, 3)])
            sc.op("v", lambda v, i=i, xbuf=xbuf, l0=l0: v.tensor_scalar(
                lnst[:, l0:l0 + D], xbuf[:, i * D:(i + 1) * D], lnsc[:, 0:1], lnsc[:, 2:3],
                ALU.subtract, ALU.mult),
                reads=[xr, R(lnsc, 0, 128, 0, 1), R(lnsc, 0, 128, 2, 3)],
                writes=[R(lnst, 0, 128, l0, l0 + D)])
            for k in range(KD):
                pt = ptrs[k % 2]
                sc.op("t", lambda t, k=k, pt=pt, l0=l0: t.transpose(
                    pt[:, 0:128], lnst[:, l0 + k * 128:l0 + (k + 1) * 128], identb_sb[:, :]),
                    reads=[R(lnst, 0, 128, l0 + k * 128, l0 + (k + 1) * 128), R(identb_sb, 0, 128, 0, 128)],
                    writes=[R(pt, 0, 128, 0, 1024)])
                sc.op("s", lambda s, k=k, i=i, pt=pt: s.activation(
                    lnT[:, k * TT + i * 128: k * TT + (i + 1) * 128],
                    pt[:, 0:128], AF.Identity, bias=0.0, scale=1.0),
                    reads=[R(pt, 0, 128, 0, 1024)],
                    writes=[R(lnT, 0, 128, k * TT + i * 128, k * TT + (i + 1) * 128)])

        def router(i):
            def rmm(t, i=i):
                ins = None
                for k in range(KD):
                    ins = t.matmul(py[3][:, 0:8],
                                   lnT[:, k * TT + i * 128: k * TT + (i + 1) * 128],
                                   gate_bf[:, k * E:(k + 1) * E],
                                   start=(k == 0), stop=(k == KD - 1))
                return ins
            sc.op("t", rmm,
                  reads=[R(lnT, 0, 128, k * TT + i * 128, k * TT + (i + 1) * 128) for k in range(KD)]
                        + [R(gate_bf, 0, 128, 0, KD * E)],
                  writes=[R(py[3], 0, 128, 0, 512)])
            sc.op("v", lambda v: v.reduce_max(lnsc[:, 3:4], py[3][:, 0:8], axis=AX, negate=True),
                  reads=[R(py[3], 0, 128, 0, 512)], writes=[R(lnsc, 0, 128, 3, 4)])
            ei0, ei1 = i * E, (i + 1) * E
            sc.op("s", lambda s, ei0=ei0, ei1=ei1: s.activation(
                e_sb[:, ei0:ei1], py[3][:, 0:8], AF.Exp, bias=lnsc[:, 3:4], scale=1.0),
                reads=[R(py[3], 0, 128, 0, 512), R(lnsc, 0, 128, 3, 4)],
                writes=[R(e_sb, 0, 128, ei0, ei1)])
            c0 = i * 4
            sc.op("v", lambda v, ei0=ei0, ei1=ei1, c0=c0: v.reduce_max(
                rsc[:, c0:c0 + 1], e_sb[:, ei0:ei1], axis=AX),
                reads=[R(e_sb, 0, 128, ei0, ei1)], writes=[R(rsc, 0, 128, c0, c0 + 1)])
            sc.op("s", lambda s, c0=c0: s.activation(
                rsc2[:, c0:c0 + 1], rsc[:, c0:c0 + 1], AF.Identity, bias=0.0, scale=1.0),
                reads=[R(rsc, 0, 128, c0, c0 + 1)], writes=[R(rsc2, 0, 128, c0, c0 + 1)])
            sc.op("v", lambda v, ei0=ei0, ei1=ei1, c0=c0: v.tensor_scalar(
                ris[:, :], e_sb[:, ei0:ei1], rsc2[:, c0:c0 + 1], None, ALU.is_ge),
                reads=[R(e_sb, 0, 128, ei0, ei1), R(rsc2, 0, 128, c0, c0 + 1)],
                writes=[R(ris, 0, 128, 0, E)])
            sc.op("v", lambda v, ei0=ei0, ei1=ei1: v.scalar_tensor_tensor(
                rwo[:, :], ris[:, :], -2.0, e_sb[:, ei0:ei1], ALU.mult, ALU.add),
                reads=[R(ris, 0, 128, 0, E), R(e_sb, 0, 128, ei0, ei1)],
                writes=[R(rwo, 0, 128, 0, E)])
            sc.op("v", lambda v, c0=c0: v.reduce_max(rsc[:, c0 + 1:c0 + 2], rwo[:, :], axis=AX),
                  reads=[R(rwo, 0, 128, 0, E)], writes=[R(rsc, 0, 128, c0 + 1, c0 + 2)])
            sc.op("v", lambda v, c0=c0: v.tensor_tensor(
                rsc[:, c0 + 2:c0 + 3], rsc[:, c0:c0 + 1], rsc[:, c0 + 1:c0 + 2], ALU.add),
                reads=[R(rsc, 0, 128, c0, c0 + 2)], writes=[R(rsc, 0, 128, c0 + 2, c0 + 3)])
            sc.op("v", lambda v, c0=c0: v.reciprocal(rsc[:, c0 + 3:c0 + 4], rsc[:, c0 + 2:c0 + 3]),
                  reads=[R(rsc, 0, 128, c0 + 2, c0 + 3)], writes=[R(rsc, 0, 128, c0 + 3, c0 + 4)])
            sc.op("s", lambda s, c0=c0: s.activation(
                rsc2[:, c0 + 1:c0 + 4], rsc[:, c0 + 1:c0 + 4], AF.Identity, bias=0.0, scale=1.0),
                reads=[R(rsc, 0, 128, c0 + 1, c0 + 4)], writes=[R(rsc2, 0, 128, c0 + 1, c0 + 4)])
            sc.op("v", lambda v, ei0=ei0, ei1=ei1, c0=c0: v.tensor_scalar(
                rkeep[:, :], e_sb[:, ei0:ei1], rsc2[:, c0 + 1:c0 + 2], None, ALU.is_ge),
                reads=[R(e_sb, 0, 128, ei0, ei1), R(rsc2, 0, 128, c0 + 1, c0 + 2)],
                writes=[R(rkeep, 0, 128, 0, E)])
            sc.op("v", lambda v, ei0=ei0, ei1=ei1, c0=c0: v.scalar_tensor_tensor(
                comb[:, ei0:ei1], e_sb[:, ei0:ei1], rsc2[:, c0 + 3:c0 + 4], rkeep[:, :],
                ALU.mult, ALU.mult),
                reads=[R(e_sb, 0, 128, ei0, ei1), R(rsc2, 0, 128, c0 + 3, c0 + 4),
                       R(rkeep, 0, 128, 0, E)],
                writes=[R(comb, 0, 128, ei0, ei1)])
            sc.op("v", lambda v, ei0=ei0, ei1=ei1: v.tensor_copy(comb_bf[:, ei0:ei1], comb[:, ei0:ei1]),
                  reads=[R(comb, 0, 128, ei0, ei1)], writes=[R(comb_bf, 0, 128, ei0, ei1)])

        def router_fin(i):
            ei0, ei1 = i * E, (i + 1) * E
            pt = ptrs[i % 2]
            sc.op("t", lambda t, ei0=ei0, ei1=ei1, i=i, pt=pt: t.transpose(
                pt[0:8, 0:128], comb_bf[:, ei0:ei1], identb_sb[:, :]),
                reads=[R(comb_bf, 0, 128, ei0, ei1), R(identb_sb, 0, 128, 0, 128)],
                writes=[R(pt, 0, 128, 0, 1024)])
            sc.op("s", lambda s, i=i, pt=pt: s.activation(
                combT[0:8, i * 128:(i + 1) * 128], pt[0:8, 0:128],
                AF.Identity, bias=0.0, scale=1.0),
                reads=[R(pt, 0, 128, 0, 1024)],
                writes=[R(combT, 0, 8, i * 128, (i + 1) * 128)])

        def b2_init():
            for idx, (i, n) in enumerate([(i, n) for i in range(NSUB) for n in range(2)]):
                bank = py[(2 * i + n) % 4]
                sc.op("t", lambda t, i=i, n=n, bank=bank: t.matmul(
                    bank[:, 0:512], combT[0:8, i * 128:(i + 1) * 128],
                    b2_bf[0:8, n * 512:(n + 1) * 512], start=True, stop=True),
                    reads=[R(combT, 0, 8, i * 128, (i + 1) * 128), R(b2_bf, 0, 8, n * 512, (n + 1) * 512)],
                    writes=[R(bank, 0, 128, 0, 512)])
                sc.op("s", lambda s, i=i, n=n, bank=bank: s.activation(
                    acc_sb[:, i * D + n * 512: i * D + (n + 1) * 512], bank[:, 0:512],
                    AF.Identity, bias=0.0, scale=1.0),
                    reads=[R(bank, 0, 128, 0, 512)],
                    writes=[R(acc_sb, 0, 128, i * D + n * 512, i * D + (n + 1) * 512)])

        w1_chunk_idx = [0]
        w2_chunk_idx = [0]

        def load_w1_chunk(t, e, q):
            par = w1_chunk_idx[0] % 2
            w1_chunk_idx[0] += 1
            dst = w1b[par]
            wreg = [R(dst, 0, 128, 0, KD * 1024)]
            bsem = ("w1a", "w1b")[par]
            if t == 0:
                src = dr3(w1_d[e, :, q * 1024:(q + 1) * 1024])
                gdma(sb3(dst[:, :], KD), src, [DR(w1_d, e * 4 + q, e * 4 + q + 1)], wreg,
                     sem=("c1a", "c1b")[par])
                sdst = dr3(w1bf_d[e, :, q * 1024:(q + 1) * 1024])
                ydma(sdst, sb3(dst[:, :], KD), wreg, [DR(w1bf_d, e * 4 + q, e * 4 + q + 1)],
                     sem=("s1a", "s1b")[par])
            else:
                src = dr3(w1bf_d[e, :, q * 1024:(q + 1) * 1024])
                ydma(sb3(dst[:, :], KD), src, [DR(w1bf_d, e * 4 + q, e * 4 + q + 1)], wreg, sem=bsem)
            return par

        def load_w2_chunk(t, e, n, q):
            par = w2_chunk_idx[0] % 2
            w2_chunk_idx[0] += 1
            dst = w2b[par]
            wreg = [R(dst, 0, 128, 0, 8 * 512)]
            ci = e * 8 + n * 4 + q
            bsem = ("w2a", "w2b")[par]
            if t == 0:
                src = dr3(w2_d[e, q * 1024:(q + 1) * 1024, n * 512:(n + 1) * 512])
                gdma(sb3(dst[:, :], 8), src, [DR(w2_d, ci, ci + 1)], wreg,
                     sem=("c2a", "c2b")[par])
                sdst = dr3(w2bf_d[e, q * 1024:(q + 1) * 1024, n * 512:(n + 1) * 512])
                ydma(sdst, sb3(dst[:, :], 8), wreg, [DR(w2bf_d, ci, ci + 1)],
                     sem=("s2a", "s2b")[par])
            else:
                src = dr3(w2bf_d[e, q * 1024:(q + 1) * 1024, n * 512:(n + 1) * 512])
                ydma(sb3(dst[:, :], 8), src, [DR(w2bf_d, ci, ci + 1)], wreg, sem=bsem)
            return par

        p1banks = [pb[0], pb[1], py[2], py[3]]

        def expert_p1(t, e):
            def p1_mm(f, par):
                bank = p1banks[f % len(p1banks)]

                def mm1(tt, par=par, fl=f % 8, bank=bank):
                    ins = None
                    for k in range(KD):
                        ins = tt.matmul(bank[:, 0:512],
                                        w1b[par][:, k * 1024 + fl * 128: k * 1024 + (fl + 1) * 128],
                                        lnT[:, k * TT:(k + 1) * TT],
                                        start=(k == 0), stop=(k == KD - 1))
                    return ins
                sc.op("t", mm1,
                      reads=[R(w1b[par], 0, 128, 0, KD * 1024), R(lnT, 0, 128, 0, KD * TT)],
                      writes=[R(bank, 0, 128, 0, 512)])

            def p1_sX(f):
                bank = p1banks[f % len(p1banks)]
                p = f % 2
                xs = slice(p * 512, (p + 1) * 512)
                xreg = R(xg, 0, 128, p * 512, (p + 1) * 512)
                bcol = f * E + e
                sc.op("s", lambda s, xs=xs, bank=bank, bcol=bcol: s.activation(
                    xg[:, xs], bank[:, 0:512], AF.Identity, bias=b1col[:, bcol:bcol + 1], scale=1.0),
                    reads=[R(bank, 0, 128, 0, 512), R(b1col, 0, 128, bcol, bcol + 1)],
                    writes=[xreg])

            def p1_vUV(f):
                p = f % 2
                xs = slice(p * 512, (p + 1) * 512)
                xreg = R(xg, 0, 128, p * 512, (p + 1) * 512)
                ureg = R(ug, 0, 128, p * 512, (p + 1) * 512)
                vreg = R(vg, 0, 128, p * 512, (p + 1) * 512)
                sc.op("v", lambda v, xs=xs: v.tensor_tensor(
                    ug[:, xs], xg[:, xs], xg[:, xs], ALU.mult),
                    reads=[xreg], writes=[ureg])
                sc.op("v", lambda v, xs=xs: v.tensor_scalar(
                    ug[:, xs], ug[:, xs], GC1, GC2, ALU.mult, ALU.add),
                    reads=[ureg], writes=[ureg])
                sc.op("v", lambda v, xs=xs: v.tensor_tensor(
                    vg[:, xs], ug[:, xs], xg[:, xs], ALU.mult),
                    reads=[ureg, xreg], writes=[vreg])

            def p1_sC(f):
                p = f % 2
                xs = slice(p * 512, (p + 1) * 512)
                vreg = R(vg, 0, 128, p * 512, (p + 1) * 512)
                sc.op("s", lambda s, xs=xs: s.activation(vg[:, xs], vg[:, xs], AF.Sigmoid),
                      reads=[vreg], writes=[vreg])

            def p1_gD(f):
                p = f % 2
                xs = slice(p * 512, (p + 1) * 512)
                sc.op("v", lambda v, xs=xs, f=f: v.tensor_tensor(
                    hT[:, f * TT:(f + 1) * TT], vg[:, xs], xg[:, xs], ALU.mult),
                    reads=[R(vg, 0, 128, p * 512, (p + 1) * 512),
                           R(xg, 0, 128, p * 512, (p + 1) * 512)],
                    writes=[R(hT, 0, 128, f * TT, (f + 1) * TT)])

            for q in range(4):
                par = load_w1_chunk(t, e, q)
                for fl in range(8):
                    f = q * 8 + fl
                    p1_mm(f, par)
                    p1_sX(f)
                    if f >= 1:
                        p1_sC(f - 1)
                    p1_vUV(f)
                    if f >= 1:
                        p1_gD(f - 1)
            p1_sC(NF - 1)
            p1_gD(NF - 1)

        def expert_p2(t, e):
            for n in range(2):
                for q in range(4):
                    par = load_w2_chunk(t, e, n, q)
                    for i in range(NSUB):

                        def mm2(tt, par=par, q=q, i=i, n=n):
                            ins = None
                            for j in range(8):
                                f = q * 8 + j
                                ins = tt.matmul(
                                    py[i][:, 0:512],
                                    hT[:, f * TT + i * 128: f * TT + (i + 1) * 128],
                                    w2b[par][:, j * 512:(j + 1) * 512],
                                    start=(q == 0 and j == 0), stop=(q == 3 and j == 7),
                                    skip_group_check=True)
                            return ins
                        sc.op("t", mm2,
                              reads=[R(hT, 0, 128, q * 8 * TT, (q + 1) * 8 * TT),
                                     R(w2b[par], 0, 128, 0, 8 * 512)],
                              writes=[R(py[i], 0, 128, 0, 512)])
                        if q == 3:
                            ccol = i * E + e
                            d0 = i * D + n * 512
                            sc.op("v", lambda v, i=i, ccol=ccol, d0=d0: v.scalar_tensor_tensor(
                                acc_sb[:, d0:d0 + 512], py[i][:, 0:512], comb[:, ccol:ccol + 1],
                                acc_sb[:, d0:d0 + 512], ALU.mult, ALU.add),
                                reads=[R(py[i], 0, 128, 0, 512), R(comb, 0, 128, ccol, ccol + 1),
                                       R(acc_sb, 0, 128, d0, d0 + 512)],
                                writes=[R(acc_sb, 0, 128, d0, d0 + 512)])

        def ffn_sub(t, i, xbuf):
            bnk = pb[i % 2]

            def rmm(tt, i=i, bnk=bnk):
                ins = None
                for k in range(KD):
                    ins = tt.matmul(bnk[0:16, 0:128],
                                    a_bf[:, k * 16:(k + 1) * 16],
                                    lnT[:, k * TT + i * 128: k * TT + (i + 1) * 128],
                                    start=(k == 0), stop=(k == KD - 1))
                return ins
            sc.op("t", rmm,
                  reads=[R(a_bf, 0, 128, 0, KD * 16)]
                        + [R(lnT, 0, 128, k * TT + i * 128, k * TT + (i + 1) * 128) for k in range(KD)],
                  writes=[R(bnk, 0, 128, 0, 512)])
            sc.op("s", lambda s, i=i, bnk=bnk: s.activation(
                rT_sb[0:16, i * 128:(i + 1) * 128], bnk[0:16, 0:128],
                AF.Identity, bias=0.0, scale=1.0),
                reads=[R(bnk, 0, 128, 0, 512)],
                writes=[R(rT_sb, 0, 16, i * 128, (i + 1) * 128)])
            for n in range(2):
                bank = py[(2 * i + n) % 4]

                def fmm(tt, i=i, n=n, bank=bank):
                    tt.matmul(bank[:, 0:512], onesb[0:1, 0:128],
                              fbrow[0:1, n * 512:(n + 1) * 512], start=True, stop=False)
                    for k in range(KD):
                        tt.matmul(bank[:, 0:512],
                                  lnT[:, k * TT + i * 128: k * TT + (i + 1) * 128],
                                  wqb[:, k * 1024 + n * 512: k * 1024 + (n + 1) * 512],
                                  start=False, stop=False)
                    return tt.matmul(bank[:, 0:512], rT_sb[0:16, i * 128:(i + 1) * 128],
                                     bb_bf[0:16, n * 512:(n + 1) * 512], start=False, stop=True)
                sc.op("t", fmm,
                      reads=[R(onesb, 0, 1, 0, 128), R(fbrow, 0, 1, n * 512, (n + 1) * 512),
                             R(wqb, 0, 128, 0, KD * 1024),
                             R(rT_sb, 0, 16, i * 128, (i + 1) * 128),
                             R(bb_bf, 0, 16, n * 512, (n + 1) * 512)]
                            + [R(lnT, 0, 128, k * TT + i * 128, k * TT + (i + 1) * 128) for k in range(KD)],
                      writes=[R(bank, 0, 128, 0, 512)])
                d0 = i * D + n * 512
                sc.op("v", lambda v, bank=bank, d0=d0, xbuf=xbuf: v.tensor_tensor(
                    out_sb[:, d0:d0 + 512], bank[:, 0:512], xbuf[:, d0:d0 + 512], ALU.add),
                    reads=[R(bank, 0, 128, 0, 512), R(xbuf, 0, 128, d0, d0 + 512)],
                    writes=[R(out_sb, 0, 128, d0, d0 + 512)])

        def load_x(t):
            xbuf = xb[t % 2]
            gdma(sb3(xbuf[:, :], NSUB), dr3(x_d[t * TT:(t + 1) * TT, :]),
                 [DR(x_d, t, t + 1)], [R(xbuf, 0, 128, 0, NSUB * D)],
                 sem=("gxa", "gxb")[t % 2])

        load_x(0)
        for i in range(NSUB):
            layernorm(i, xb[0])
        preamble_b1()
        preamble_quant()
        for t in range(nt):
            x_sb = xb[t % 2]
            if do_router:
                for i in range(NSUB):
                    router(i)
            else:
                for idx in range(8):
                    i, n = idx // 2, idx % 2
                    sc.op("v", lambda v, i=i, n=n: v.memset(
                        acc_sb[:, i * D + n * 512: i * D + (n + 1) * 512], 0.0),
                        reads=[], writes=[R(acc_sb, 0, 128, i * D + n * 512, i * D + (n + 1) * 512)])
                sc.op("v", lambda v: v.memset(comb[:, :], 0.25),
                      reads=[], writes=[R(comb, 0, 128, 0, NSUB * E)])
                sc.op("v", lambda v: v.tensor_copy(comb_bf[:, :], comb[:, :]),
                      reads=[R(comb, 0, 128, 0, NSUB * E)], writes=[R(comb_bf, 0, 128, 0, NSUB * E)])
            for e in range(ne):
                expert_p1(t, e)
                if e == 0:
                    for i in range(NSUB):
                        router_fin(i)
                    b2_init()
                expert_p2(t, e)
            for i in range(NSUB):
                sc.op("v", lambda v, i=i, x_sb=x_sb: v.tensor_tensor(
                    x_sb[:, i * D:(i + 1) * D], x_sb[:, i * D:(i + 1) * D],
                    acc_sb[:, i * D:(i + 1) * D], ALU.add),
                    reads=[R(x_sb, 0, 128, i * D, (i + 1) * D), R(acc_sb, 0, 128, i * D, (i + 1) * D)],
                    writes=[R(x_sb, 0, 128, i * D, (i + 1) * D)])
                if do_ffn:
                    layernorm(i, x_sb)
                    ffn_sub(t, i, x_sb)
            if t + 1 < nt:
                load_x(t + 1)
                for i in range(NSUB):
                    layernorm(i, xb[(t + 1) % 2])
            if do_debug and t == 0:
                gdma(dbg["dbg_x1"], x_sb[:, :], [R(x_sb, 0, 128, 0, 4096)], [], sem="dbg")
            if not do_ffn:
                for idx in range(8):
                    i, n = idx // 2, idx % 2
                    d0 = i * D + n * 512
                    sc.op("v", lambda v, d0=d0: v.tensor_copy(
                        out_sb[:, d0:d0 + 512], x_sb[:, d0:d0 + 512]),
                        reads=[R(x_sb, 0, 128, d0, d0 + 512)],
                        writes=[R(out_sb, 0, 128, d0, d0 + 512)])
            gdma(dr3(y_d[t * TT:(t + 1) * TT, :]), sb3(out_sb[:, :], NSUB),
                 [R(out_sb, 0, 128, 0, NSUB * D)], [DR(y_d, t, t + 1)], sem="gy")

        if do_debug:
            if do_router:
                gdma(dbg["dbg_comb"], comb[:, :], [R(comb, 0, 128, 0, 32)], [], sem="dbg")
                gdma(dbg["dbg_e"], e_sb[:, :], [R(e_sb, 0, 128, 0, 32)], [], sem="dbg")
                sc.op("v", lambda v: v.tensor_copy(xg[0:8, 0:512], combT[0:8, :]),
                      reads=[R(combT, 0, 8, 0, 512)], writes=[R(xg, 0, 8, 0, 512)])
                gdma(dbg["dbg_combT"], xg[0:8, 0:512], [R(xg, 0, 8, 0, 512)], [], sem="dbg")
                gdma(dbg["dbg_rsc"], rsc[:, :], [R(rsc, 0, 128, 0, 16)], [], sem="dbg")
            gdma(dbg["dbg_acc"], acc_sb[:, :], [R(acc_sb, 0, 128, 0, 4096)], [], sem="dbg")
            sc.op("v", lambda v: v.tensor_copy(xg[:, 0:512], rT_sb[0:16, :].rearrange("p (a c) -> p a c", a=1)[:, 0, :]) if False else v.tensor_copy(ug[0:16, 0:512], rT_sb[0:16, :]),
                  reads=[R(rT_sb, 0, 16, 0, TT)], writes=[R(ug, 0, 16, 0, 512)])
            gdma(dbg["dbg_rT"], ug[0:16, 0:512], [R(ug, 0, 16, 0, 512)], [], sem="dbg")
            sc.op("v", lambda v: v.tensor_copy(vg[:, 0:512], wqb[:, 0:512]),
                  reads=[R(wqb, 0, 128, 0, 512)], writes=[R(vg, 0, 128, 0, 512)])
            gdma(dbg["dbg_wq"][:, 0:512], vg[:, 0:512], [R(vg, 0, 128, 0, 512)], [], sem="dbg")
            gdma(dbg["dbg_qsc"], qsc[:, :], [R(qsc, 0, 128, 0, 2)], [], sem="dbg")
            gdma(dbg["dbg_lnT"], lnT[:, :].rearrange("p (a c) -> p a c", a=KD),
                 [R(lnT, 0, 128, 0, KD * TT)], [], sem="dbg")
            if ne > 0:
                gdma(dbg["dbg_hT"], hT[:, 0:2048].rearrange("p (a c) -> p a c", a=4),
                     [R(hT, 0, 128, 0, 2048)], [], sem="dbg")

        for smm in Sched.DMA_SEMS:
            if sc.cnt[smm] > 0:
                sc.raw_wait("g", smm, sc.cnt[smm])

        with nc.Block() as blk:
            @blk.tensor
            def _(engine):
                sc.replay("t", engine, sems)

            @blk.vector
            def _(engine):
                sc.replay("v", engine, sems)

            @blk.scalar
            def _(engine):
                sc.replay("s", engine, sems)

            @blk.gpsimd
            def _(engine):
                sc.replay("g", engine, sems)

            @blk.sync
            def _(engine):
                sc.replay("y", engine, sems)

    return nc


_NC_CACHE = {}


def kernel(L0_attn_hidden_states, L0_attn_gate_w, L0_attn_w1, L0_attn_b1,
           L0_attn_w2, L0_attn_b2, L0_ffn_kernel, L0_ffn_bias, L0_ffn_a, L0_ffn_b):
    x = np.ascontiguousarray(np.asarray(L0_attn_hidden_states, dtype=np.float32))
    xf = x.reshape(B * S, D)
    gate = np.ascontiguousarray(np.asarray(L0_attn_gate_w, dtype=np.float32))
    w1 = np.ascontiguousarray(np.asarray(L0_attn_w1, dtype=np.float32))
    b1 = np.ascontiguousarray(np.asarray(L0_attn_b1, dtype=np.float32))
    w2 = np.ascontiguousarray(np.asarray(L0_attn_w2, dtype=np.float32))
    b2 = np.ascontiguousarray(np.asarray(L0_attn_b2, dtype=np.float32))
    fk = np.ascontiguousarray(np.asarray(L0_ffn_kernel, dtype=np.float32))
    fb = np.ascontiguousarray(np.asarray(L0_ffn_bias, dtype=np.float32)).reshape(1, D)
    fa = np.ascontiguousarray(np.asarray(L0_ffn_a, dtype=np.float32))
    fbb = np.ascontiguousarray(np.asarray(L0_ffn_b, dtype=np.float32))
    identb = np.eye(128, dtype=ml_dtypes.bfloat16)
    identf = np.eye(128, dtype=np.float32)
    consts = np.tile(np.array([[1e-5, -0.5, GC1, GC2]], dtype=np.float32), (128, 1))
    onesf = np.ones((1, 512), dtype=np.float32)

    if "nc" not in _NC_CACHE:
        _NC_CACHE["nc"] = build_program()
    nc = _NC_CACHE["nc"]

    in_maps = []
    for c in range(NCORES):
        in_maps.append({
            "x": np.ascontiguousarray(xf[c * TC:(c + 1) * TC]),
            "gate": gate, "w1": w1, "b1": b1, "w2": w2, "b2": b2,
            "fk": fk, "fb": fb, "fa": fa, "fbb": fbb,
            "identb": identb, "identf": identf, "consts": consts, "onesf": onesf,
        })
    res = run_bass_kernel_spmd(nc, in_maps, core_ids=list(range(NCORES)))
    out = np.concatenate([res.results[c]["y"] for c in range(NCORES)], axis=0)
    return out.reshape(B, S, D)
